# version 18
# baseline (speedup 1.0000x reference)
import numpy as np
import ml_dtypes

import concourse.bass as bass
import concourse.mybir as mybir
from concourse.bass_utils import run_bass_kernel_spmd

F32 = mybir.dt.float32
BF16 = mybir.dt.bfloat16
I32 = mybir.dt.int32
AF = mybir.ActivationFunctionType
ALU = mybir.AluOpType
AX = mybir.AxisListType

NCORES = 8
NB = 4
S = 2048
D = 1024
NH = 16
NPAIR = 8
NST = 16
NCT = 8
FF = 2048
NEXP = 8
NAV = 128
EPS_LN = 1e-5
EPS_SM = 1e-8
CHUNK = 512
CHUNK_ORDER = (3, 0, 1, 2)

DBG = []
FUSE_PV = True


def _host_consts():
    i = np.arange(32, dtype=np.float64)
    inv = 10000.0 ** (-2.0 * i / 64.0)
    pos = np.arange(S, dtype=np.float64)
    ang = pos[None, :] * inv[:, None]
    cos = np.cos(ang)
    sin = np.sin(ang)
    cosT = np.empty((128, S), np.float32)
    sinT = np.empty((128, S), np.float32)
    for p_ in range(128):
        k = (p_ % 64) // 2
        cosT[p_] = cos[k]
        sinT[p_] = sin[k] if (p_ % 2) else -sin[k]
    perm = np.zeros((128, 128), np.float32)
    for r in range(0, 128, 2):
        perm[r + 1, r] = 1.0
        perm[r, r + 1] = 1.0
    tri = (np.arange(128)[:, None] <= np.arange(128)[None, :]).astype(np.float32)
    tri2 = np.concatenate([tri, tri], axis=1).astype(ml_dtypes.bfloat16)
    bsel = np.zeros((48, 8 * 128), np.float32)
    for base in (0, 32):
        for t in range(8):
            bsel[base + 2 * t, t * 128: t * 128 + 64] = 1.0
            bsel[base + 2 * t + 1, t * 128 + 64: (t + 1) * 128] = 1.0
    return dict(cosT=cosT, sinT=sinT,
                ident_bf=np.eye(128).astype(ml_dtypes.bfloat16),
                ident_f=np.eye(128).astype(np.float32),
                perm_bf=perm.astype(ml_dtypes.bfloat16), tri2=tri2,
                iota8=np.arange(8, dtype=np.float32)[None, :],
                bsel=bsel.astype(ml_dtypes.bfloat16))


STREAMS = ("tensor", "scalar", "vector", "gpsimd", "sync")
CSEM = {"tensor": "pe", "scalar": "act", "vector": "dve", "gpsimd": "pool", "sync": "syn"}
DSEM = {"scalar": "dsc", "gpsimd": "dgp", "sync": "dsy"}
NDG = {"sync": 8, "scalar": 8, "gpsimd": 4}
SEMS = ("pe", "act", "dve", "pool") + tuple(
    f"{DSEM[s]}{i}" for s in NDG for i in range(NDG[s]))


class Prog:

    def __init__(self):
        self.q = {s: [] for s in STREAMS}
        self.cnt = {}
        self.dma_n = {}
        self.dma_tok = {}
        self.last_w = {}
        self.readers = {}

    def add(self, stream, fn, r=(), w=(), deps=(), dma=False, hard=False):
        dep = set(d for d in deps if d is not None)
        if dma:
            base = DSEM[stream]
            ndg = NDG[stream]
            n = self.dma_n.get(stream, 0)
            self.dma_n[stream] = n + 1
            sem = f"{base}{n % ndg}"
            inc = 16
            if n >= ndg:
                dep.add(self.dma_tok[(stream, (n - ndg) % ndg)])
        else:
            sem = CSEM[stream]
            inc = 1
        cur = self.cnt.get(sem, 0) + inc
        self.cnt[sem] = cur
        tok = (sem, cur)
        if dma:
            self.dma_tok[(stream, (self.dma_n[stream] - 1) % NDG[stream])] = tok
        for res in r:
            if res in self.last_w:
                dep.add(self.last_w[res])
        for res in w:
            if res in self.last_w:
                dep.add(self.last_w[res])
            for t in self.readers.get(res, ()):
                dep.add(t)
        for res in r:
            self.readers.setdefault(res, []).append(tok)
        for res in w:
            self.last_w[res] = tok
            self.readers[res] = []
        dep.discard(tok)
        self.q[stream].append((sorted(dep), fn, sem, inc, hard))
        return tok

    def cur(self, res):
        toks = list(self.readers.get(res, ()))
        if res in self.last_w:
            toks.append(self.last_w[res])
        return toks

    def emit(self, nc, semh):
        totals = dict(self.cnt)
        with nc.Block() as block:
            for stream in STREAMS:
                tasks = self.q[stream]
                own = CSEM[stream]

                def body(eng, tasks=tasks, own=own, stream=stream):
                    guar = {}
                    nown = 0
                    for deps, fn, sem, inc, hard in tasks:
                        for (s2, v2) in deps:
                            if guar.get(s2, 0) < v2:
                                eng.wait_ge(semh[s2], v2)
                                guar[s2] = v2
                        fn(eng).then_inc(semh[sem], inc)
                        if sem == own:
                            nown += 1
                            guar[own] = max(guar.get(own, 0), 0)
                    if stream == "sync":
                        for s2 in SEMS:
                            if s2 in totals and guar.get(s2, 0) < totals[s2]:
                                eng.wait_ge(semh[s2], totals[s2])

                getattr(block, stream)(body)


def build_program(nb=NB, dbg=()):
    nc = bass.Bass("TRN2", target_bir_lowering=False, debug=False)

    x_in = nc.dram_tensor("x", [nb, S, D], F32, kind="ExternalInput")
    W = {}
    for n, shp in [("Wq", [D, D]), ("bq", [D]), ("Wk", [D, D]), ("bk", [D]),
                   ("Wv", [D, D]), ("bv", [D]), ("Wo", [D, D]), ("bo", [D]),
                   ("Wp1", [D, NAV]), ("bp1", [NAV]), ("Wp2", [NAV, NEXP]), ("bp2", [NEXP]),
                   ("W1", [NEXP, D, FF]), ("b1", [NEXP, FF]), ("W2", [NEXP, FF, D]),
                   ("b2", [NEXP, D]), ("WoF", [D, D]), ("boF", [D])]:
        W[n] = nc.dram_tensor(n, shp, F32, kind="ExternalInput")
    C = {}
    for n, shp, dt in [("cosT", [128, S], F32), ("sinT", [128, S], F32),
                       ("ident_bf", [128, 128], BF16), ("ident_f", [128, 128], F32),
                       ("perm_bf", [128, 128], BF16), ("tri2", [128, 256], BF16),
                       ("bsel", [48, 8 * 128], BF16),
                       ("iota8", [1, 8], F32)]:
        C[n] = nc.dram_tensor(n, shp, dt, kind="ExternalInput")
    out_d = nc.dram_tensor("out", [nb, S, D], F32, kind="ExternalOutput")

    x1_scr = nc.dram_tensor("x1_scr", [S, D], F32, kind="Internal")
    xlt_scr = nc.dram_tensor("xlt_scr", [NCT, 128, S], BF16, kind="Internal")
    wt_scr = nc.dram_tensor("wt_scr", [4, 128, NCT * D], BF16, kind="Internal")
    wtF_scr = nc.dram_tensor("wtF_scr", [128, NCT * D], BF16, kind="Internal")

    dbg_out = {}
    for name, shp, dt in [
        ("d_xlnT", [128, NCT * S], BF16), ("d_QT", [128, NCT * S], BF16),
        ("d_KT", [128, NCT * S], BF16), ("d_Vt", [128, NST * 1040], BF16),
        ("d_ctx3", [128, NPAIR * CHUNK], BF16), ("d_sums3", [16, CHUNK], F32),
        ("d_beta3", [16, CHUNK], F32), ("d_ctxf3", [128, NPAIR * CHUNK], BF16),
        ("d_x1", [S, D], F32), ("d_row", [1, 8], F32), ("d_sel", [1, 1], I32),
        ("d_w1s", [128, NCT * FF], BF16), ("d_h0", [128, NH * CHUNK], BF16),
        ("d_xlt", [NCT, 128, S], BF16), ("d_lastln", [128, NCT], F32),
        ("d_b1c", [128, 16], F32), ("d_b2c", [128, NEXP], F32),
        ("d_xin0", [128, NCT * CHUNK], BF16), ("d_rs0", [128, NCT * CHUNK], BF16),
        ("d_hraw", [128, CHUNK], F32), ("d_b1g", [16, 128], F32),
        ("d_b1c_early", [128, 16], F32), ("d_r2", [128, NST], F32),
        ("d_m2r2", [128, NST], F32), ("d_cb", [128, 16], F32),
        ("d_pdsnap", [128, 16], F32), ("d_b1gsnap", [16, 128], F32),
        ("d_idf", [128, 128], F32), ("d_idb", [128, 128], BF16),
    ]:
        if name in dbg:
            dbg_out[name] = nc.dram_tensor(name, shp, dt, kind="ExternalOutput")

    off = [16384 + 128]

    def place(name, shape, dtype, at=None):
        esz = 2 if dtype == BF16 else 4
        nbytes = int(np.prod(shape[1:])) * esz
        if at is None:
            at = off[0]
            off[0] = at + ((nbytes + 31) // 32) * 32
        return nc.alloc_sbuf_tensor_at(name, shape, dtype, offset=at)

    R1 = place("R1", [128, NCT * S], BF16)
    R2 = place("R2", [128, NCT * S], BF16)
    R3 = place("R3", [128, NCT * S], BF16)
    R4 = place("R4", [128, NST * 1040], BF16)
    R5 = place("R5", [128, NCT * D], BF16)
    r6 = off[0]
    R6c = place("R6c", [128, NPAIR * CHUNK], BF16)
    R6p = place("R6p", [128, 2048], BF16)
    R6tab = place("R6tab", [128, S], F32, at=r6)
    r7 = off[0]
    R7f = place("R7f", [128, NPAIR * CHUNK], BF16)
    R7tab = place("R7tab", [128, S], F32, at=r7)
    off[0] = max(off[0], r7 + 4 * S)
    R8a = place("R8a", [128, D], F32)
    R8b = place("R8b", [128, D], F32)
    R9 = place("R9", [128, 2 * S], BF16)
    R10a = place("R10a", [128, D], F32)
    R10b = place("R10b", [128, D], F32)
    X1L = place("X1L", [128, NCT * CHUNK], BF16)
    x1l_at = X1L.manual_sbuf_range[0] if X1L.manual_sbuf_range else 0
    SMb1g = nc.alloc_sbuf_tensor_at("SM_b1g", [16, 128], F32, offset=x1l_at + 2048)
    SMb2g = nc.alloc_sbuf_tensor_at("SM_b2g", [NEXP, 128], F32, offset=x1l_at + 2560)
    SMbvrow = nc.alloc_sbuf_tensor_at("SM_bvrow", [1, D], BF16, offset=X1L.manual_sbuf_range[0]
                                      if X1L.manual_sbuf_range else 0)

    SM = {}
    for n, shp, dt in [
        ("r2", [128, NST], F32), ("m2r2", [128, NST], F32), ("r2p1", [128, NST], F32),
        ("stats", [128, 24], F32), ("mv", [128, 4], F32), ("ln1", [128, 6], F32),
        ("bqc", [128, NCT], F32), ("bkc", [128, NCT], F32), ("boc", [128, NCT], F32),
        ("boFc", [128, NCT], F32), ("b2c", [128, NCT], F32),
        ("cb0", [128, 8], F32), ("b1c", [128, 16], F32), ("cb1", [128, 8], F32),
        ("bp1", [128, 1], F32), ("wp2", [128, NEXP], F32), ("bp2", [NEXP, 1], F32),
        ("tri", [128, 256], BF16), ("identb", [128, 128], BF16),
        ("identf", [128, 128], F32), ("perm", [128, 128], BF16), ("iota8", [1, 8], F32),
        ("ones1b", [1, 128], BF16), ("ones1f", [1, 128], F32),
        ("row", [1, 8], F32), ("row2", [1, 8], F32), ("row3", [1, 8], F32),
        ("m", [1, 1], F32), ("umin", [1, 1], F32), ("self", [1, 1], F32), ("seli", [1, 1], I32),
        ("g", [128, 1], F32), ("lastT", [128, NCT], F32), ("lastln", [128, NCT], F32),
        ("epsln", [128, 1], F32), ("epssm", [128, 1], F32),
        ("bsel", [48, 8 * 128], BF16), ("bhi", [16, CHUNK], BF16),
        ("sc1", [1, 2], F32), ("bc1", [128, 2], F32),
        ("logc", [NEXP, 1], F32),
        ("xln2", [128, D], BF16), ("xln2b", [128, D], BF16),
    ]:
        SM[n] = place("SM_" + n, shp, dt)
    assert off[0] <= 16384 + 212576 - 64, f"sbuf overflow: {off[0]}"

    PA = nc.alloc_psum_tensor("PA", [128, 1024], F32)
    PB = nc.alloc_psum_tensor("PB", [128, 1024], F32)
    PC = nc.alloc_psum_tensor("PC", [128, 1024], F32)
    PD = nc.alloc_psum_tensor("PD", [128, 512], F32)
    PE2 = nc.alloc_psum_tensor("PE2", [128, 512], F32)
    PDb = PD.bitcast(BF16)
    PE2b = PE2.bitcast(BF16)

    p = Prog()
    from contextlib import ExitStack
    stack = ExitStack()
    semh = {s: stack.enter_context(nc.semaphore(s)) for s in SEMS}
    greg1 = stack.enter_context(nc.gpsimd.register("greg1"))

    p.add("sync", lambda e: e.dma_start(SM["tri"][:, :], C["tri2"][:, :]), w=["tri"], dma=True)
    p.add("sync", lambda e: e.dma_start(SM["identb"][:, :], C["ident_bf"][:, :]), w=["identb"], dma=True)
    p.add("sync", lambda e: e.dma_start(SM["identf"][:, :], C["ident_f"][:, :]), w=["identf"], dma=True)
    p.add("sync", lambda e: e.dma_start(SM["perm"][:, :], C["perm_bf"][:, :]), w=["perm"], dma=True)
    p.add("sync", lambda e: e.dma_start(SM["iota8"][:, :], C["iota8"][:, :]), w=["iota8"], dma=True)
    p.add("sync", lambda e: e.dma_start(SM["bsel"][:, :], C["bsel"][:, :]), w=["bsel"], dma=True)
    p.add("sync", lambda e: e.dma_start(
        SM["bp1"][:, :], W["bp1"].ap().rearrange("(p o) -> p o", o=1)), w=["bp1"], dma=True)
    p.add("sync", lambda e: e.dma_start(SM["wp2"][:, :], W["Wp2"][:, :]), w=["wp2"], dma=True)
    p.add("sync", lambda e: e.dma_start(
        SM["bp2"][:, :], W["bp2"].ap().rearrange("(p o) -> p o", o=1)), w=["bp2"], dma=True)
    p.add("vector", lambda e: e.memset(SM["ones1b"][:, :], 1.0), w=["ones1b"])
    p.add("vector", lambda e: e.memset(SM["epsln"][:, :], EPS_LN), w=["epsln"])
    p.add("vector", lambda e: e.memset(SM["epssm"][:, :], EPS_SM), w=["epssm"])
    p.add("vector", lambda e: e.memset(SM["ones1f"][:, :], 1.0), w=["ones1f"])

    for i, wkey in enumerate(("Wq", "Wk", "Wv", "Wo")):
        p.add("gpsimd", lambda e, wkey=wkey: e.dma_start(
            R5[:, :].rearrange("q (dt c) -> q dt c", dt=NCT),
            W[wkey].ap().rearrange("(dt q) c -> q dt c", q=128)),
            w=["R5"], dma=True)
        for ct in range(NCT):
            pb = PDb if ct % 2 == 0 else PE2b
            pk = "PD" if ct % 2 == 0 else "PE2"

            def tf(e, ct=ct, pb=pb):
                for dt in range(NCT):
                    m = e.transpose(pb[:, dt * 128:(dt + 1) * 128],
                                    R5[:, dt * D + ct * 128: dt * D + (ct + 1) * 128],
                                    SM["identb"][:, :])
                return m
            p.add("tensor", tf, r=["R5", "identb"], w=[pk])
            p.add("scalar", lambda e, ct=ct, pb=pb: e.copy(
                R3[:, ct * D:(ct + 1) * D], pb[:, 0:1024]), r=[pk], w=["R3"])
        p.add("scalar", lambda e, i=i: e.dma_start(wt_scr[i, :, :], R3[:, 0:NCT * D]),
              r=["R3"], w=[f"wt{i}"], dma=True)

    R5f32 = R5.bitcast(F32)
    for hh in range(2):
        p.add("sync", lambda e, hh=hh: e.dma_start(
            R5f32[:, 0:4096].rearrange("q (ht ee) -> q ht ee", ht=4),
            W["WoF"].ap().rearrange("(ht q) ee -> q ht ee", q=128)[:, hh * 4:hh * 4 + 4, :]),
            w=["R5"], dma=True)
        p.add("vector", lambda e: e.tensor_copy(R3[:, 0:4096], R5f32[:, 0:4096]),
              r=["R5"], w=["R3"])
        p.add("scalar", lambda e, hh=hh: e.dma_start(
            wtF_scr[:, hh * 4096:(hh + 1) * 4096], R3[:, 0:4096]),
            r=["R3"], w=["wtF"], dma=True)

    for bkey, dstk in (("bq", "bqc"), ("bk", "bkc"), ("bo", "boc"), ("boF", "boFc")):
        p.add("sync", lambda e, bkey=bkey: e.dma_start(
            R10a[0:1, 0:1024], W[bkey].ap().rearrange("(o d) -> o d", o=1)),
            w=["R10a"], dma=True)

        def tf(e):
            for ct in range(NCT):
                m = e.transpose(PD[:, ct:ct + 1], R10a[0:1, ct * 128:(ct + 1) * 128],
                                SM["identf"][0:1, 0:1])
            return m
        p.add("tensor", tf, r=["R10a", "identf"], w=["PD"])
        p.add("scalar", lambda e, dstk=dstk: e.copy(SM[dstk][:, 0:NCT], PD[:, 0:NCT]),
              r=["PD"], w=[dstk])

    for b in range(nb):
        for st in range(NST):
            buf, bk_ = (R8a, "R8a") if st % 2 == 0 else (R8b, "R8b")
            bkq = [bk_ + ("L", "L2", "H", "H2")[i] for i in range(4)]
            for qi in range(4):
                eng = "sync"
                p.add(eng, lambda e, b=b, st=st, buf=buf, qi=qi: e.dma_start(
                    buf[qi * 32:(qi + 1) * 32, :],
                    x_in[b, st * 128 + qi * 32: st * 128 + (qi + 1) * 32, :]),
                    w=[bkq[qi]], dma=True)

            q_ = st % 2
            XL = SM["xln2"] if q_ == 0 else SM["xln2b"]
            sq, mq, lq = 12 * q_, 2 * q_, 3 * q_
            rs = lambda n: n + str(q_)

            def stf(e, buf=buf, sq=sq):
                e.bn_stats(SM["stats"][:, sq:sq + 6], buf[:, 0:512])
                return e.bn_stats(SM["stats"][:, sq + 6:sq + 12], buf[:, 512:1024])
            p.add("vector", stf, r=bkq, w=[rs("stats")])
            p.add("vector", lambda e, sq=sq, mq=mq: e.bn_aggr(
                SM["mv"][:, mq:mq + 2], SM["stats"][:, sq:sq + 12]),
                r=[rs("stats")], w=[rs("mv")], hard=True)
            p.add("scalar", lambda e, mq=mq, lq=lq: e.activation(
                SM["ln1"][:, lq + 2:lq + 3], SM["mv"][:, mq + 1:mq + 2], AF.Ln,
                bias=SM["epsln"][:, 0:1]),
                r=[rs("mv"), "epsln"], w=[rs("ln1x")], hard=True)
            p.add("scalar", lambda e, lq=lq: e.activation(
                SM["ln1"][:, lq:lq + 1], SM["ln1"][:, lq + 2:lq + 3], AF.Exp, scale=-0.5),
                r=[rs("ln1x")], w=[rs("ln1")], hard=True)
            p.add("vector", lambda e, mq=mq, lq=lq: e.scalar_tensor_tensor(
                SM["ln1"][:, lq + 1:lq + 2], SM["mv"][:, mq:mq + 1], -1.0,
                SM["ln1"][:, lq:lq + 1],
                ALU.mult, ALU.mult), r=[rs("mv"), rs("ln1")], w=[rs("ln1b")], hard=True)
            p.add("scalar", lambda e, buf=buf, lq=lq, XL=XL: e.activation(
                XL[:, :], buf[:, :], AF.Identity,
                bias=SM["ln1"][:, lq + 1:lq + 2], scale=SM["ln1"][:, lq:lq + 1]),
                r=bkq + [rs("ln1"), rs("ln1b")], w=[rs("xln2")], hard=True)
            pb, pk = (PDb, "PD") if st % 2 == 0 else (PE2b, "PE2")

            def tf(e, pb=pb, XL=XL):
                for ct in range(NCT):
                    m = e.transpose(pb[:, ct * 128:(ct + 1) * 128],
                                    XL[:, ct * 128:(ct + 1) * 128], SM["identb"][:, :])
                return m
            p.add("tensor", tf, r=[rs("xln2"), "identb"], w=[pk])
            p.add("vector", lambda e, st=st, pb=pb: e.tensor_copy(
                R1[:, :].rearrange("q (ct s) -> q ct s", ct=NCT)[:, :, st * 128:(st + 1) * 128],
                pb[:, 0:1024].rearrange("q (ct s) -> q ct s", ct=NCT)),
                r=[pk], w=["R1h" + str(st // 8)])

        if b == 0 and "d_xlnT" in dbg_out:
            p.add("sync", lambda e: e.dma_start(dbg_out["d_xlnT"][:, :], R1[:, :]),
                  r=["R1h0", "R1h1"], dma=True)

        p.add("sync", lambda e: e.dma_start(R6tab[:, :], C["cosT"][:, :]),
              w=["R6tab", "R6c"], dma=True)
        p.add("sync", lambda e: e.dma_start(R7tab[:, :], C["sinT"][:, :]),
              w=["R7tab", "R7f"], dma=True)

        def proj_rope(Wi, dstT, dstks, bck):
            p.add("sync", lambda e, Wi=Wi: e.dma_start(R5[:, :], wt_scr[Wi, :, :]),
                  r=[f"wt{Wi}"], w=["R5"], dma=True)

            def emit_perm_rope(unit):
                dt, qq = unit // 2, unit % 2
                o = 1024 * (unit % 2)
                rk = "R9q" + str(unit % 2)
                if unit % 2 == 0:
                    pd_ap, pe_ap, pdk, pek = PD[:, 0:512], PE2[:, 0:512], "PD", "PE2"
                else:
                    pd_ap, pe_ap, pdk, pek = PC[:, 0:512], PC[:, 512:1024], "PC0", "PC1"

                def permf(e, o=o, pd_ap=pd_ap, pe_ap=pe_ap):
                    e.matmul(pd_ap, SM["perm"][:, :], R9[:, o:o + 512],
                             start=True, stop=True)
                    return e.matmul(pe_ap, SM["perm"][:, :], R9[:, o + 512:o + 1024],
                                    start=True, stop=True)
                p.add("tensor", permf, r=[rk, "perm"], w=[pdk, pek])
                scol = qq * 1024

                def ropef1(e, o=o, scol=scol, pd_ap=pd_ap, pe_ap=pe_ap):
                    e.tensor_mul(R10a[:, 0:1024], R9[:, o:o + 1024],
                                 R6tab[:, scol:scol + 1024])
                    e.tensor_mul(R10b[:, 0:512], pd_ap, R7tab[:, scol:scol + 512])
                    return e.tensor_mul(R10b[:, 512:1024], pe_ap,
                                        R7tab[:, scol + 512:scol + 1024])
                p.add("vector", ropef1, r=[rk, pdk, pek, "R6tab", "R7tab"],
                      w=["R10a", "R10b", "R10bs0", "R10bs1", "R10bln0", "R10bln1",
                         "bhi", "blo"])
                p.add("vector", lambda e, dt=dt, scol=scol: e.tensor_add(
                    dstT[:, dt * S + scol: dt * S + scol + 1024],
                    R10a[:, 0:1024], R10b[:, 0:1024]),
                    r=["R10a", "R10b"], w=dstks(dt), hard=True)

            for dt in range(NCT):
                for qq in range(2):
                    unit = dt * 2 + qq
                    pb, pk = (PA, "PA") if unit % 2 == 0 else (PB, "PB")

                    def mf(e, dt=dt, qq=qq, pb=pb):
                        for ct in range(NCT):
                            for nn in range(2):
                                m = e.matmul(
                                    pb[:, nn * 512:(nn + 1) * 512],
                                    R5[:, ct * D + dt * 128: ct * D + (dt + 1) * 128],
                                    R1[:, ct * S + qq * 1024 + nn * 512:
                                       ct * S + qq * 1024 + (nn + 1) * 512],
                                    start=(ct == 0), stop=(ct == NCT - 1))
                        return m
                    p.add("tensor", mf, r=["R5", "R1h" + str(qq)], w=[pk + "0", pk + "1", pk])
                    o = 1024 * (unit % 2)
                    rk = "R9q" + str(unit % 2)
                    p.add("scalar", lambda e, pb=pb, dt=dt, o=o, bck=bck: e.activation(
                        R9[:, o:o + 1024], pb[:, 0:1024], AF.Identity,
                        bias=SM[bck][:, dt:dt + 1]),
                        r=[pk, pk + "0", pk + "1", bck], w=[rk])
                    if unit >= 1:
                        emit_perm_rope(unit - 1)
            emit_perm_rope(15)

        proj_rope(0, R2, lambda dt: ["R2p%d" % dt], "bqc")
        proj_rope(1, R3,
                  lambda dt: ["R3", "R3h0", "R3h1", "R3h2", "R3h3", "R3r"], "bkc")

        p.add("vector", lambda e: e.memset(
            R4[:, :].rearrange("q (t h c) -> q t h c", t=NST, h=NH)[:, :, :, 64:65], 1.0),
            w=["R4"])
        p.add("gpsimd", lambda e: e.dma_start(
            SMbvrow[:, :], W["bv"].ap().rearrange("(o d) -> o d", o=1)), w=["X1L"], dma=True)
        p.add("sync", lambda e: e.dma_start(R5[:, :], wt_scr[2, :, :]),
              r=["wt2"], w=["R5"], dma=True)
        for st in range(NST):
            pbv, pk = (PA, "PA") if st % 2 == 0 else (PB, "PB")

            def vf(e, st=st, pbv=pbv):
                for ct in range(NCT):
                    for nn in range(2):
                        e.matmul(pbv[:, nn * 512:(nn + 1) * 512],
                                 R1[:, ct * S + st * 128: ct * S + (st + 1) * 128],
                                 R5[:, ct * D + nn * 512: ct * D + (nn + 1) * 512],
                                 start=(ct == 0), stop=False)
                for nn in range(2):
                    m = e.matmul(pbv[:, nn * 512:(nn + 1) * 512], SM["ones1b"][:, :],
                                 SMbvrow[:, nn * 512:(nn + 1) * 512], start=False, stop=True)
                return m
            p.add("tensor", vf, r=["R5", "R1h" + str(st // 8), "ones1b", "X1L"],
                  w=[pk, pk + "0", pk + "1"])
            p.add("vector", lambda e, st=st, pbv=pbv: e.tensor_copy(
                R4[:, :].rearrange("q (t h c) -> q t h c", t=NST, h=NH)[:, st, :, 0:64],
                pbv[:, 0:1024].rearrange("q (h c) -> q h c", h=NH)),
                r=[pk, pk + "0", pk + "1"], w=["R4"])

        if b == 0 and "d_QT" in dbg_out:
            p.add("sync", lambda e: e.dma_start(dbg_out["d_QT"][:, :], R2[:, :]),
                  r=["R2p%d" % i for i in range(NPAIR)], dma=True)
        if b == 0 and "d_KT" in dbg_out:
            p.add("sync", lambda e: e.dma_start(dbg_out["d_KT"][:, :], R3[:, :]), r=["R3"], dma=True)
        if b == 0 and "d_Vt" in dbg_out:
            p.add("sync", lambda e: e.dma_start(dbg_out["d_Vt"][:, :], R4[:, :]), r=["R4"], dma=True)

        p.add("sync", lambda e: e.dma_start(R5[:, :], wt_scr[3, :, :]),
              r=["wt3"], w=["R5"], dma=True)

        R10bb = R10b.bitcast(BF16)
        tail_tt = {}
        tail_beta = [None]

        def emit_attn(c, hooks=None, mid_hooks=None):
            pi = 32 * (c % 2)
            for t in range(NPAIR):
                njs = 4 * c + 4

                def sc_mm(e, t, j, qoff, w_, stb):
                    kb = t * S + j * 128
                    qb = t * S + c * CHUNK + qoff
                    e.matmul(stb[:, qoff:CHUNK], R3[0:64, kb:kb + 128],
                             R2[0:64, qb:qb + w_], start=True, stop=True,
                             tile_position=(0, 0))
                    return e.matmul(stb[:, CHUNK + qoff:2 * CHUNK], R3[64:128, kb:kb + 128],
                                    R2[64:128, qb:qb + w_], start=True, stop=True,
                                    tile_position=(64, 0))

                def pv_mm(e, t, j, qoff, w_, po, first, lastj):
                    vb = j * 1040 + (2 * t) * 65
                    e.matmul(PC[0:65, qoff:CHUNK], R4[:, vb:vb + 65],
                             R6p[:, po + qoff:po + qoff + w_], start=first, stop=lastj)
                    return e.matmul(PC[0:65, CHUNK + qoff:2 * CHUNK],
                                    R4[:, vb + 65:vb + 130],
                                    R6p[:, po + CHUNK + qoff:po + CHUNK + qoff + w_],
                                    start=first, stop=lastj)

                def piece(j):
                    qoff = max(0, 128 * j - CHUNK * c)
                    return qoff, CHUNK - qoff, (PA, "PA") if j % 2 == 0 else (PB, "PB")

                for j in (0, 1):
                    qoff, w_, (stb, sk) = piece(j)
                    p.add("tensor",
                          lambda e, t=t, j=j, qoff=qoff, w_=w_, stb=stb:
                          sc_mm(e, t, j, qoff, w_, stb),
                          r=["R2p%d" % t, "R3"], w=[sk + "0", sk + "1", sk])
                for j in range(njs):
                    qoff, w_, (stb, sk) = piece(j)
                    po = 1024 * (j % 2)
                    pkk = "R6p" + str(j % 2)

                    def ef(e, qoff=qoff, stb=stb, po=po):
                        if qoff == 0:
                            return e.activation(R6p[:, po:po + 1024], stb[:, 0:1024],
                                                AF.Exp, scale=0.125)
                        src_ = stb[:, 0:1024].rearrange("q (bb z) -> q bb z", bb=2)[:, :, qoff:CHUNK]
                        dst = R6p[:, po:po + 1024].rearrange(
                            "q (bb z) -> q bb z", bb=2)[:, :, qoff:CHUNK]
                        return e.activation(dst, src_, AF.Exp, scale=0.125)
                    p.add("scalar", ef, r=[sk, sk + "0", sk + "1"], w=[pkk])
                    if 128 * j >= CHUNK * c:
                        def mkf(e, qoff=qoff, po=po):
                            dst = R6p[:, po:po + 1024].rearrange(
                                "q (bb z) -> q bb z", bb=2)[:, :, qoff:qoff + 128]
                            src_ = SM["tri"][:, :].rearrange("q (bb z) -> q bb z", bb=2)
                            return e.tensor_tensor(dst, dst, src_, ALU.mult)
                        p.add("vector", mkf, r=["tri", pkk], w=[pkk])
                    jn = j + 2
                    if FUSE_PV and jn < njs:
                        qoff2, w2_, (stb2, sk2_) = piece(jn)

                        def fused(e, t=t, j=j, jn=jn, qoff=qoff, w_=w_, po=po,
                                  first=(j == 0), lastj=(j == njs - 1),
                                  qoff2=qoff2, w2_=w2_, stb2=stb2):
                            pv_mm(e, t, j, qoff, w_, po, first, lastj)
                            return sc_mm(e, t, jn, qoff2, w2_, stb2)
                        p.add("tensor", fused, r=[pkk, "R4", "R2p%d" % t, "R3"],
                              w=["PC0", "PC1", sk2_ + "0", sk2_ + "1", sk2_])
                    else:
                        if not FUSE_PV and jn < njs:
                            qoff2, w2_, (stb2, sk2_) = piece(jn)
                            p.add("tensor",
                                  lambda e, jn=jn, qoff2=qoff2, w2_=w2_, stb2=stb2, t=t:
                                  sc_mm(e, t, jn, qoff2, w2_, stb2),
                                  r=["R2p%d" % t, "R3"], w=[sk2_ + "0", sk2_ + "1", sk2_])
                        p.add("tensor",
                              lambda e, t=t, j=j, qoff=qoff, w_=w_, po=po,
                              first=(j == 0), lastj=(j == njs - 1):
                              pv_mm(e, t, j, qoff, w_, po, first, lastj),
                              r=[pkk, "R4"], w=["PC0", "PC1"])
                if mid_hooks and t in mid_hooks:
                    mid_hooks[t]()
                dep_tt = tail_tt.get(t)
                p.add("vector", lambda e, t=t: e.tensor_copy(
                    R6c[0:64, t * CHUNK:(t + 1) * CHUNK], PC[0:64, 0:CHUNK]),
                    r=["PC0"], w=["R6c", "R6tab"], deps=[dep_tt])
                p.add("vector", lambda e, t=t: e.tensor_copy(
                    R6c[64:128, t * CHUNK:(t + 1) * CHUNK], PC[0:64, CHUNK:2 * CHUNK]),
                    r=["PC1"], w=["R6c"], deps=[dep_tt])
                p.add("vector", lambda e: e.tensor_copy(
                    R10b[64:65, 0:2 * CHUNK], PC[64:65, 0:2 * CHUNK]),
                    r=["PC0", "PC1", "R10b"], w=["R7tA"])
                p.add("sync", lambda e, t=t, pi=pi: e.dma_start(
                    R10b[pi + 2 * t:pi + 2 * t + 1, 0:CHUNK], R10b[64:65, 0:CHUNK]),
                    r=["R7tA", "R10b"], w=["R10bs" + str(c % 2)],
                    deps=[tail_beta[0]], dma=True)
                p.add("sync", lambda e, t=t, pi=pi: e.dma_start(
                    R10b[pi + 2 * t + 1:pi + 2 * t + 2, 0:CHUNK],
                    R10b[64:65, CHUNK:2 * CHUNK]),
                    r=["R7tA", "R10b"], w=["R10bs" + str(c % 2)],
                    deps=[tail_beta[0]], dma=True)
                if hooks and t in hooks:
                    hooks[t]()

        def emit_tailA_beta(c):
            pi = 32 * (c % 2)
            sk2 = "R10bs" + str(c % 2)
            kln = "R10bln" + str(c % 2)
            BHI = R10bb[pi:pi + 16, 1024:1024 + CHUNK]
            BLO = R10bb[pi:pi + 16, 1024 + CHUNK:1024 + 2 * CHUNK]
            p.add("scalar", lambda e, pi=pi: e.activation(
                R10b[pi:pi + 16, CHUNK:2 * CHUNK], R10b[pi:pi + 16, 0:CHUNK], AF.Ln,
                bias=SM["epssm"][0:16, 0:1]),
                r=[sk2, "epssm"], w=[kln], hard=True)
            p.add("scalar", lambda e, pi=pi: e.activation(
                R10b[pi:pi + 16, 0:CHUNK], R10b[pi:pi + 16, CHUNK:2 * CHUNK],
                AF.Exp, scale=-1.0),
                r=[kln], w=[sk2], hard=True)

            if b == 0 and c == 3:
                if "d_ctx3" in dbg_out:
                    p.add("sync", lambda e: e.dma_start(dbg_out["d_ctx3"][:, :], R6c[:, :]),
                          r=["R6c"], dma=True)
                if "d_beta3" in dbg_out:
                    p.add("sync", lambda e, pi=pi: e.dma_start(
                        dbg_out["d_beta3"][:, :], R10b[pi:pi + 16, 0:CHUNK]),
                        r=[sk2], dma=True)

            p.add("scalar", lambda e, pi=pi, BHI=BHI: e.activation(
                BHI, R10b[pi:pi + 16, 0:CHUNK], AF.Identity),
                r=[sk2], w=["bhi", kln], hard=True)
            tail_beta[0] = p.add("vector", lambda e, pi=pi, BHI=BHI, BLO=BLO: e.tensor_sub(
                BLO, R10b[pi:pi + 16, 0:CHUNK], BHI),
                r=[sk2, "bhi"], w=["blo", kln], hard=True)
            return pi, BHI, BLO

        def emit_tailA_bcf(c, pi, BHI, BLO, t):
            pbc, pck = (PD, "PD") if t % 2 == 0 else (PE2, "PE2")

            def bcf(e, t=t, pbc=pbc, pi=pi, BHI=BHI, BLO=BLO):
                e.matmul(pbc[:, 0:CHUNK], SM["bsel"][pi:pi + 16, t * 128:(t + 1) * 128],
                         BHI, start=True, stop=False)
                return e.matmul(pbc[:, 0:CHUNK], SM["bsel"][pi:pi + 16, t * 128:(t + 1) * 128],
                                BLO, start=False, stop=True)
            p.add("tensor", bcf, r=["bhi", "blo", "bsel"], w=[pck])
            tail_tt[t] = p.add("vector", lambda e, t=t, pbc=pbc, pck=pck: e.tensor_tensor(
                R7f[:, t * CHUNK:(t + 1) * CHUNK],
                R6c[:, t * CHUNK:(t + 1) * CHUNK],
                pbc[:, 0:CHUNK], ALU.mult),
                r=["R6c", pck], w=["R7f", "R7tab"])

        def emit_tailA(c):
            pi, BHI, BLO = emit_tailA_beta(c)
            for t in range(NPAIR):
                emit_tailA_bcf(c, pi, BHI, BLO, t)

            if b == 0 and c == 3 and "d_ctxf3" in dbg_out:
                p.add("sync", lambda e: e.dma_start(dbg_out["d_ctxf3"][:, :], R7f[:, :]),
                      r=["R7f"], dma=True)

        def emit_tailB(c):
            for et in range(NCT):
                pp, pk = (PD, "PD") if et % 2 == 0 else (PE2, "PE2")

                def opf(e, et=et, pp=pp):
                    for pt in range(NCT):
                        m = e.matmul(pp[:, 0:512],
                                     R5[:, pt * D + et * 128:pt * D + (et + 1) * 128],
                                     R7f[:, pt * CHUNK:(pt + 1) * CHUNK],
                                     start=(pt == 0), stop=(pt == NCT - 1))
                    return m
                p.add("tensor", opf, r=["R5", "R7f"], w=[pk])
                p.add("vector", lambda e, et=et, pp=pp: e.tensor_scalar_add(
                    R9[:, et * 512:(et + 1) * 512], pp[:, 0:512],
                    SM["boc"][:, et:et + 1]),
                    r=[pk, "boc"], w=["R9q" + str(et // 2)])

            for u in range(4):
                st = 4 * c + u
                pbj, pjk = (PDb, "PD") if u % 2 == 0 else (PE2b, "PE2")

                def trf(e, u=u, pbj=pbj):
                    for et in range(NCT):
                        m = e.transpose(pbj[:, et * 128:(et + 1) * 128],
                                        R9[:, et * 512 + u * 128: et * 512 + (u + 1) * 128],
                                        SM["identb"][:, :])
                    return m
                p.add("tensor", trf, r=["R9q0", "R9q1", "R9q2", "R9q3", "identb"], w=[pjk])
                for qi in range(4):
                    eng = ("sync", "scalar", "sync", "scalar")[qi]
                    res = ("R8aL", "R8aL2", "R8aH", "R8aH2")[qi]
                    for hh in range(2):
                        p.add(eng, lambda e, b=b, c=c, u=u, qi=qi, hh=hh: e.dma_start(
                            R8a[qi * 32 + hh * 16: qi * 32 + hh * 16 + 16, :],
                            x_in[b, c * CHUNK + u * 128 + qi * 32 + hh * 16:
                                 c * CHUNK + u * 128 + qi * 32 + (hh + 1) * 16, :]),
                            w=[res + ("x" if hh else "")], dma=True)
                p.add("vector", lambda e, pbj=pbj: e.tensor_tensor(
                    R8b[:, :], R8a[:, :], pbj[:, 0:1024], ALU.add),
                    r=["R8aL", "R8aL2", "R8aH", "R8aH2",
                       "R8aLx", "R8aL2x", "R8aHx", "R8aH2x"] + [pjk],
                    w=["R8bL", "R8bL2", "R8bH", "R8bH2"])
                p.add("scalar", lambda e, c=c, u=u: e.dma_start(
                    x1_scr[c * CHUNK + u * 128: c * CHUNK + u * 128 + 64, :], R8b[0:64, :]),
                    r=["R8bL", "R8bL2"], w=["x1s" + str(st)], dma=True)
                p.add("sync", lambda e, c=c, u=u: e.dma_start(
                    x1_scr[c * CHUNK + u * 128 + 64: c * CHUNK + (u + 1) * 128, :],
                    R8b[64:128, :]),
                    r=["R8bH", "R8bH2"], w=["x1sB" + str(st)], dma=True)

                def stf2(e):
                    e.bn_stats(SM["stats"][:, 0:6], R8b[:, 0:512])
                    return e.bn_stats(SM["stats"][:, 6:12], R8b[:, 512:1024])
                p.add("vector", stf2, r=["R8bL", "R8bL2", "R8bH", "R8bH2"], w=["stats0"])
                p.add("vector", lambda e: e.bn_aggr(SM["mv"][:, 0:2], SM["stats"][:, 0:12]),
                      r=["stats0"], w=["mv0"], hard=True)
                p.add("scalar", lambda e: e.activation(
                    SM["ln1"][:, 2:3], SM["mv"][:, 1:2], AF.Ln, bias=SM["epsln"][:, 0:1]),
                    r=["mv0", "epsln"], w=["ln1x0"], hard=True)
                p.add("scalar", lambda e, st=st: e.activation(
                    SM["r2"][:, st:st + 1], SM["ln1"][:, 2:3], AF.Exp, scale=-0.5),
                    r=["ln1x0"], w=["r2"], hard=True)
                p.add("vector", lambda e, st=st: e.scalar_tensor_tensor(
                    SM["m2r2"][:, st:st + 1], SM["mv"][:, 0:1], -1.0, SM["r2"][:, st:st + 1],
                    ALU.mult, ALU.mult), r=["mv0", "r2"], w=["m2r2"], hard=True)
                p.add("scalar", lambda e, st=st: e.activation(
                    SM["xln2"][:, :], R8b[:, :], AF.Identity,
                    bias=SM["m2r2"][:, st:st + 1], scale=SM["r2"][:, st:st + 1]),
                    r=["R8bL", "R8bL2", "R8bH", "R8bH2"] + ["r2", "m2r2"],
                    w=["xln20"], hard=True)
                pbx, pxk = (PE2b, "PE2") if u % 2 == 0 else (PDb, "PD")

                def txf(e, pbx=pbx):
                    for ct in range(NCT):
                        m = e.transpose(pbx[:, ct * 128:(ct + 1) * 128],
                                        SM["xln2"][:, ct * 128:(ct + 1) * 128],
                                        SM["identb"][:, :])
                    return m
                p.add("tensor", txf, r=["xln20", "identb"], w=[pxk])
                p.add("vector", lambda e, u=u, pbx=pbx: e.tensor_copy(
                    X1L[:, :].rearrange("q (ct z) -> q ct z", ct=NCT)[:, :, u * 128:(u + 1) * 128],
                    pbx[:, 0:1024].rearrange("q (ct z) -> q ct z", ct=NCT)),
                    r=[pxk], w=["X1L"])
                if c == 3 and u == 3:
                    p.add("sync", lambda e: e.dma_start(R10a[0:1, 0:1024], R8b[127:128, :]),
                          r=["R8bL", "R8bL2", "R8bH", "R8bH2"], w=["R10a"], dma=True)
            p.add("vector", lambda e, c=c: e.tensor_scalar_add(
                SM["r2p1"][:, 4 * c:4 * c + 4], SM["r2"][:, 4 * c:4 * c + 4], 1.0),
                r=["r2"], w=["r2p1"], hard=True)
            p.add("scalar", lambda e, c=c: e.dma_start(
                xlt_scr[:, :, c * CHUNK:(c + 1) * CHUNK].rearrange("ct q z -> q ct z"),
                X1L[:, :].rearrange("q (ct z) -> q ct z", ct=NCT)),
                r=["X1L"], w=["xlt" + str(c)], dma=True)

        def router_stages():
          def s1():
            def rtr_t1(e):
                for ct in range(NCT):
                    m = e.transpose(PD[:, ct:ct + 1], R10a[0:1, ct * 128:(ct + 1) * 128],
                                    SM["identf"][0:1, 0:1])
                return m
            p.add("tensor", rtr_t1, r=["R10a", "identf"], w=["PD"])
            p.add("scalar", lambda e: e.copy(SM["lastT"][:, :], PD[:, 0:NCT]),
                  r=["PD"], w=["lastT"], hard=True)
            p.add("sync", lambda e: e.dma_start(SM["sc1"][0:1, 0:1], SM["r2"][127:128, 15:16]),
                  r=["r2"], w=["sc1"], dma=True)
            p.add("sync", lambda e: e.dma_start(SM["sc1"][0:1, 1:2], SM["m2r2"][127:128, 15:16]),
                  r=["m2r2"], w=["sc1"], dma=True)
            p.add("tensor", lambda e: e.matmul(
                PE2[:, 0:2], SM["ones1f"][:, :], SM["sc1"][0:1, 0:2], start=True, stop=True),
                r=["sc1", "ones1f"], w=["PE2"])
            p.add("scalar", lambda e: e.copy(SM["bc1"][:, :], PE2[:, 0:2]),
                  r=["PE2"], w=["bc1"], hard=True)
            p.add("scalar", lambda e: e.activation(
                SM["lastln"][:, :], SM["lastT"][:, :], AF.Identity,
                bias=SM["bc1"][:, 1:2], scale=SM["bc1"][:, 0:1]),
                r=["lastT", "bc1"], w=["lastln"], hard=True)
            if b == 0 and "d_lastln" in dbg_out:
                p.add("sync", lambda e: e.dma_start(dbg_out["d_lastln"][:, :], SM["lastln"][:, :]),
                      r=["lastln"], dma=True)
            p.add("sync", lambda e: e.dma_start(
                R10a[:, :].rearrange("q (ct n) -> q ct n", ct=NCT),
                W["Wp1"].ap().rearrange("(ct q) n -> q ct n", q=128)),
                w=["R10a"], dma=True)

          def s2():
            def rtr_mm(e):
                for ct in range(NCT):
                    m = e.matmul(PD[:, 0:1], R10a[:, ct * 128:(ct + 1) * 128],
                                 SM["lastln"][:, ct:ct + 1], start=(ct == 0),
                                 stop=(ct == NCT - 1))
                return m
            p.add("tensor", rtr_mm, r=["R10a", "lastln"], w=["PD"])
            p.add("scalar", lambda e: e.activation(
                SM["g"][:, :], PD[:, 0:1], AF.Gelu, bias=SM["bp1"][:, 0:1]),
                r=["PD", "bp1"], w=["g"], hard=True)
            p.add("tensor", lambda e: e.matmul(
                PE2[0:NEXP, 0:1], SM["wp2"][:, :], SM["g"][:, :], start=True, stop=True),
                r=["wp2", "g"], w=["PE2"])
            p.add("scalar", lambda e: e.activation(
                SM["logc"][:, :], PE2[0:NEXP, 0:1], AF.Identity, bias=SM["bp2"][:, 0:1]),
                r=["PE2", "bp2"], w=["logc"], hard=True)

          def s3():
            p.add("tensor", lambda e: e.transpose(
                PD[0:1, 0:NEXP], SM["logc"][:, :], SM["identf"][0:NEXP, 0:NEXP]),
                r=["logc", "identf"], w=["PD"])
            p.add("scalar", lambda e: e.copy(SM["row"][:, :], PD[0:1, 0:NEXP]),
                  r=["PD"], w=["row"], hard=True)
            p.add("vector", lambda e: e.tensor_reduce(
                SM["m"][:, :], SM["row"][:, :], axis=AX.X, op=ALU.max),
                r=["row"], w=["mr"], hard=True)
            p.add("vector", lambda e: e.tensor_scalar(
                SM["row2"][:, :], SM["row"][:, :], SM["m"][0:1, 0:1], -1.0,
                ALU.subtract, ALU.mult), r=["row", "mr"], w=["row2"], hard=True)
            p.add("vector", lambda e: e.scalar_tensor_tensor(
                SM["row3"][:, :], SM["iota8"][:, :], 1e-20, SM["row2"][:, :],
                ALU.mult, ALU.add),
                r=["row2", "iota8"], w=["row3"], hard=True)
            p.add("vector", lambda e: e.tensor_reduce(
                SM["umin"][:, :], SM["row3"][:, :], axis=AX.X, op=ALU.min),
                r=["row3"], w=["umin"], hard=True)
            p.add("vector", lambda e: e.tensor_single_scalar(
                SM["row2"][:, :], SM["row3"][:, :], SM["umin"][0:1, 0:1], ALU.is_equal),
                r=["row3", "umin"], w=["row2"], hard=True)
            p.add("vector", lambda e: e.tensor_mul(
                SM["row3"][:, :], SM["row2"][:, :], SM["iota8"][:, :]),
                r=["row2", "iota8"], w=["row3"], hard=True)
            p.add("vector", lambda e: e.reduce_sum(
                SM["self"][:, :], SM["row3"][:, :], axis=AX.X), r=["row3"], w=["self"], hard=True)
            p.add("vector", lambda e: e.tensor_copy(SM["seli"][:, :], SM["self"][:, :]),
                  r=["self"], w=["seli"], hard=True)
            if b == 0 and "d_row" in dbg_out:
                p.add("sync", lambda e: e.dma_start(dbg_out["d_row"][:, :], SM["row"][:, :]),
                      r=["row"], dma=True)
            if b == 0 and "d_sel" in dbg_out:
                p.add("sync", lambda e: e.dma_start(dbg_out["d_sel"][:, :], SM["seli"][:, :]),
                      r=["seli"], dma=True)

          def s4():
            def g_w1(e):
                e.reg_load(greg1, SM["seli"][0:1, 0:1])
                e.reg_mul(greg1, greg1, D * FF)
                return e.dma_start(
                    R1[:, :].rearrange("q (ct f) -> q ct f", ct=NCT),
                    bass.AP(W["W1"], greg1, [[FF, 128], [128 * FF, NCT], [1, FF]]))
            p.add("gpsimd", g_w1, r=["seli"], w=["R1", "R1h0", "R1h1"], dma=True)

            def g_b1(e):
                e.reg_load(greg1, SM["seli"][0:1, 0:1])
                e.reg_mul(greg1, greg1, FF)
                return e.dma_start(SMb1g[:, :], bass.AP(W["b1"], greg1, [[128, 16], [1, 128]]))
            p.add("gpsimd", g_b1, r=["seli"], w=["b1g", "X1L"], dma=True)

            def g_b2(e):
                e.reg_load(greg1, SM["seli"][0:1, 0:1])
                e.reg_mul(greg1, greg1, D)
                return e.dma_start(SMb2g[:, :], bass.AP(W["b2"], greg1, [[128, NEXP], [1, 128]]))
            p.add("gpsimd", g_b2, r=["seli"], w=["b2g", "X1L"], dma=True)

          def s5():
            p.add("tensor", lambda e: e.transpose(
                PE2[:, 0:16], SMb1g[:, :], SM["identf"][0:16, 0:16]),
                r=["b1g", "identf"], w=["PE2"])
            p.add("scalar", lambda e: e.copy(SM["b1c"][:, :], PE2[:, 0:16]),
                  r=["PE2"], w=["b1c"], hard=True)
            p.add("tensor", lambda e: e.transpose(
                PE2[:, 0:NEXP], SMb2g[:, :], SM["identf"][0:NEXP, 0:NEXP]),
                r=["b2g", "identf"], w=["PE2"])
            p.add("scalar", lambda e: e.copy(SM["b2c"][:, :], PE2[:, 0:NEXP]),
                  r=["PE2"], w=["b2c"], hard=True)
          return {0: s1, 1: s2, 2: s3, 3: s4, 5: s5}

        XIN0 = NCT * D

        def emit_ffn_load(k):
            xo = XIN0 + (k % 2) * (NCT * CHUNK)
            xka, xkb = "R4x%da" % (k % 2), "R4x%db" % (k % 2)
            toksR4 = p.cur("R4")
            p.add("sync", lambda e, k=k, xo=xo: e.dma_start(
                R4[:, xo:xo + 4 * CHUNK].rearrange("q (ct z) -> q ct z", ct=4),
                xlt_scr[0:4, :, k * CHUNK:(k + 1) * CHUNK].rearrange("ct q z -> q ct z")),
                r=["xlt" + str(k)], w=[xka, "R4"], dma=True)
            p.add("scalar", lambda e, k=k, xo=xo: e.dma_start(
                R4[:, xo + 4 * CHUNK: xo + 8 * CHUNK].rearrange("q (ct z) -> q ct z", ct=4),
                xlt_scr[4:8, :, k * CHUNK:(k + 1) * CHUNK].rearrange("ct q z -> q ct z")),
                r=["xlt" + str(k)], w=[xkb], deps=toksR4, dma=True)

        def emit_ffn(k, pre=None):
            xo = XIN0 + (k % 2) * (NCT * CHUNK)
            xk = ["R4x%da" % (k % 2), "R4x%db" % (k % 2)]
            for ft in range(16):
                hh = (ft % 2) * 512
                hk = "PA" + str(ft % 2)

                def hf(e, ft=ft, hh=hh, xo=xo):
                    for ct in range(NCT):
                        m = e.matmul(PA[:, hh:hh + 512],
                                     R1[:, ct * FF + ft * 128: ct * FF + (ft + 1) * 128],
                                     R4[:, xo + ct * CHUNK: xo + (ct + 1) * CHUNK],
                                     start=(ct == 0), stop=(ct == NCT - 1))
                    return m
                p.add("tensor", hf, r=["R1"] + xk, w=[hk])
                p.add("scalar", lambda e, ft=ft, hh=hh: e.activation(
                    R3[:, ft * CHUNK:(ft + 1) * CHUNK], PA[:, hh:hh + 512], AF.Gelu,
                    bias=SM["b1c"][:, ft:ft + 1]), r=[hk, "b1c"], w=["R3h" + str(ft % 4)])
            if pre is not None:
                emit_ffn_load(pre)
            if b == 0 and k == 0 and "d_h0" in dbg_out:
                p.add("sync", lambda e: e.dma_start(dbg_out["d_h0"][:, :], R3[:, 0:NH * CHUNK]),
                      r=["R3h0", "R3h1", "R3h2", "R3h3"], dma=True)
            for ht in range(NCT):
                rr = (ht % 2) * 512
                rk = "PB" + str(ht % 2)

                def rfx(e, ht=ht, rr=rr):
                    for ft in range(16):
                        m = e.matmul(PB[:, rr:rr + 512],
                                     R2[:, ft * D + ht * 128: ft * D + (ht + 1) * 128],
                                     R3[:, ft * CHUNK:(ft + 1) * CHUNK],
                                     start=(ft == 0), stop=(ft == 15))
                    return m
                p.add("tensor", rfx,
                      r=["R2p%d" % i for i in range(NPAIR)] + ["R3h0", "R3h1", "R3h2", "R3h3"],
                      w=[rk])
                p.add("vector", lambda e, ht=ht, rr=rr: e.tensor_scalar_add(
                    R3[:, NH * CHUNK + ht * CHUNK: NH * CHUNK + (ht + 1) * CHUNK],
                    PB[:, rr:rr + 512], SM["b2c"][:, ht:ht + 1]),
                    r=[rk, "b2c"], w=["R3r"])
            for et in range(NCT):
                oo = (et % 2) * 512
                ok = "PC" + str(et % 2)

                def of(e, et=et, oo=oo):
                    for ht in range(NCT):
                        m = e.matmul(PC[:, oo:oo + 512],
                                     R4[:, ht * D + et * 128: ht * D + (et + 1) * 128],
                                     R3[:, NH * CHUNK + ht * CHUNK:
                                        NH * CHUNK + (ht + 1) * CHUNK],
                                     start=(ht == 0), stop=(ht == NCT - 1))
                    return m
                p.add("tensor", of, r=["R4", "R4w1", "R4w2", "R3r"], w=[ok])
                p.add("vector", lambda e, et=et, oo=oo: e.tensor_scalar_add(
                    R9[:, et * 512:(et + 1) * 512], PC[:, oo:oo + 512],
                    SM["boFc"][:, et:et + 1]),
                    r=[ok, "boFc"], w=["R9q" + str(et // 2)])
            for u in range(4):
                st = 4 * k + u
                pbj, pjk = (PDb, "PD") if u % 2 == 0 else (PE2b, "PE2")

                def trf2(e, u=u, pbj=pbj):
                    for et in range(NCT):
                        m = e.transpose(pbj[:, et * 128:(et + 1) * 128],
                                        R9[:, et * 512 + u * 128: et * 512 + (u + 1) * 128],
                                        SM["identb"][:, :])
                    return m
                p.add("tensor", trf2, r=["R9q0", "R9q1", "R9q2", "R9q3", "identb"], w=[pjk])
                for qi in range(4):
                    eng = ("sync", "scalar", "sync", "scalar")[qi]
                    res = ("R8aL", "R8aL2", "R8aH", "R8aH2")[qi]
                    p.add(eng, lambda e, k=k, u=u, qi=qi: e.dma_start(
                        R8a[qi * 32:(qi + 1) * 32, :],
                        x1_scr[k * CHUNK + u * 128 + qi * 32:
                               k * CHUNK + u * 128 + (qi + 1) * 32, :]),
                        r=["x1s" + str(st), "x1sB" + str(st)], w=[res], dma=True)
                p.add("scalar", lambda e, st=st: e.activation(
                    R8b[:, :], R8a[:, :], AF.Identity, bias=SM["m2r2"][:, st:st + 1],
                    scale=SM["r2p1"][:, st:st + 1]),
                    r=["R8aL", "R8aL2", "R8aH", "R8aH2"] + ["m2r2", "r2p1"], w=["R8bL", "R8bL2", "R8bH", "R8bH2"])
                p.add("vector", lambda e, pbj=pbj: e.scalar_tensor_tensor(
                    R8b[:, :], pbj[:, 0:1024], 0.3, R8b[:, :], ALU.mult, ALU.add),
                    r=[pjk] + ["R8bL", "R8bL2", "R8bH", "R8bH2"], w=["R8bL", "R8bL2", "R8bH", "R8bH2"])
                p.add("scalar", lambda e, b=b, k=k, u=u: e.dma_start(
                    out_d[b, k * CHUNK + u * 128: k * CHUNK + u * 128 + 64, :], R8b[0:64, :]),
                    r=["R8bL", "R8bL2"], dma=True)
                p.add("sync", lambda e, b=b, k=k, u=u: e.dma_start(
                    out_d[b, k * CHUNK + u * 128 + 64: k * CHUNK + (u + 1) * 128, :],
                    R8b[64:128, :]),
                    r=["R8bH", "R8bH2"], dma=True)

        emit_attn(3)
        beta3 = emit_tailA_beta(3)
        emit_attn(0, mid_hooks={
            t: (lambda t=t: emit_tailA_bcf(3, beta3[0], beta3[1], beta3[2], t))
            for t in range(NPAIR)})
        if b == 0 and "d_ctxf3" in dbg_out:
            p.add("sync", lambda e: e.dma_start(dbg_out["d_ctxf3"][:, :], R7f[:, :]),
                  r=["R7f"], dma=True)
        emit_tailB(3)
        beta0 = emit_tailA_beta(0)
        emit_attn(1, hooks=router_stages(), mid_hooks={
            t: (lambda t=t: emit_tailA_bcf(0, beta0[0], beta0[1], beta0[2], t))
            for t in range(NPAIR)})
        emit_tailB(0)
        beta1 = emit_tailA_beta(1)
        def g_w2_half(h):
            def g(e, h=h):
                e.reg_load(greg1, SM["seli"][0:1, 0:1])
                e.reg_mul(greg1, greg1, FF * D)
                e.reg_add(greg1, greg1, h * 8 * 128 * D)
                return e.dma_start(
                    R2[:, h * 8192:(h + 1) * 8192].rearrange("q (ft d) -> q ft d", ft=8),
                    bass.AP(W["W2"], greg1, [[D, 128], [128 * D, 8], [1, D]]))
            p.add("gpsimd", g, r=["seli"],
                  w=["R2p%d" % t for t in range(h * 4, h * 4 + 4)], dma=True)

        def attn2_hook(t):
            emit_tailA_bcf(1, beta1[0], beta1[1], beta1[2], t)
            if t == 3:
                g_w2_half(0)
            elif t == 7:
                g_w2_half(1)
        emit_attn(2, mid_hooks={t: (lambda t=t: attn2_hook(t)) for t in range(NPAIR)})
        emit_tailB(1)
        emit_tailA(2)
        toks_pv = p.cur("R4")
        p.add("sync", lambda e: e.dma_start(R4[:, 0:4096], wtF_scr[:, 0:4096]),
              r=["wtF"], w=["R4w1"], deps=toks_pv, dma=True)
        p.add("scalar", lambda e: e.dma_start(R4[:, 4096:NCT * D], wtF_scr[:, 4096:NCT * D]),
              r=["wtF"], w=["R4w2"], deps=toks_pv, dma=True)
        if b == 0 and "d_w1s" in dbg_out:
            p.add("sync", lambda e: e.dma_start(dbg_out["d_w1s"][:, :], R1[:, :]),
                  r=["R1"], dma=True)
        if b == 0 and "d_x1" in dbg_out:
            p.add("sync", lambda e: e.dma_start(dbg_out["d_x1"][:, :], x1_scr[:, :]),
                  r=[f"x1s{st}" for st in range(16)], dma=True)
        if b == 0 and "d_xlt" in dbg_out:
            p.add("sync", lambda e: e.dma_start(dbg_out["d_xlt"][:, :, :], xlt_scr[:, :, :]),
                  r=["xlt0", "xlt1", "xlt2", "xlt3"], dma=True)

        emit_ffn_load(3)
        emit_ffn(3, pre=0)
        emit_tailB(2)
        emit_ffn(0, pre=1)
        emit_ffn(1, pre=2)
        emit_ffn(2)

    p.emit(nc, semh)
    stack.close()
    return nc, sorted(dbg_out.keys())


_CACHE = {}


def _get_prog(nb, dbg):
    key = (nb, tuple(sorted(dbg)))
    if key not in _CACHE:
        _CACHE[key] = build_program(nb, dbg)
    return _CACHE[key]


def _in_maps(inputs, nb, ncores):
    cst = _host_consts()
    x = np.ascontiguousarray(np.asarray(inputs["L0_attn_x"], dtype=np.float32))
    base = {
        "Wq": inputs["L0_attn_Wq"], "bq": inputs["L0_attn_bq"],
        "Wk": inputs["L0_attn_Wk"], "bk": inputs["L0_attn_bk"],
        "Wv": inputs["L0_attn_Wv"], "bv": inputs["L0_attn_bv"],
        "Wo": inputs["L0_attn_Wo"], "bo": inputs["L0_attn_bo"],
        "Wp1": inputs["L0_ffn_Wp1"], "bp1": inputs["L0_ffn_bp1"],
        "Wp2": inputs["L0_ffn_Wp2"], "bp2": inputs["L0_ffn_bp2"],
        "W1": inputs["L0_ffn_W1"], "b1": inputs["L0_ffn_b1"],
        "W2": inputs["L0_ffn_W2"], "b2": inputs["L0_ffn_b2"],
        "WoF": inputs["L0_ffn_Wo"], "boF": inputs["L0_ffn_bo"],
    }
    base = {k: np.ascontiguousarray(np.asarray(v, dtype=np.float32)) for k, v in base.items()}
    base.update(cst)
    maps = []
    for cc in range(ncores):
        m = dict(base)
        m["x"] = np.ascontiguousarray(x[cc * nb:(cc + 1) * nb])
        maps.append(m)
    return maps


def kernel(**inputs):
    nc, _ = _get_prog(NB, tuple(DBG))
    maps = _in_maps(inputs, NB, NCORES)
    res = run_bass_kernel_spmd(nc, maps, core_ids=list(range(NCORES)))
    out = np.concatenate([np.asarray(r["out"]) for r in res.results], axis=0)
    return np.ascontiguousarray(out.astype(np.float32))


def _run_debug(inputs, nb=1, dbg=(), ncores=1):
    nc, names = _get_prog(nb, tuple(dbg))
    maps = _in_maps(inputs, nb, ncores)
    res = run_bass_kernel_spmd(nc, maps, core_ids=list(range(ncores)))
    return res.results[0]


# version 19
# speedup vs baseline: 1.0335x; 1.0335x over previous
import numpy as np
import ml_dtypes

import concourse.bass as bass
import concourse.mybir as mybir
from concourse.bass_utils import run_bass_kernel_spmd

F32 = mybir.dt.float32
BF16 = mybir.dt.bfloat16
I32 = mybir.dt.int32
AF = mybir.ActivationFunctionType
ALU = mybir.AluOpType
AX = mybir.AxisListType

NCORES = 8
NB = 4
S = 2048
D = 1024
NH = 16
NPAIR = 8
NST = 16
NCT = 8
FF = 2048
NEXP = 8
NAV = 128
EPS_LN = 1e-5
EPS_SM = 1e-8
CHUNK = 512
CHUNK_ORDER = (3, 0, 1, 2)

DBG = []
FUSE_PV = True


def _host_consts():
    i = np.arange(32, dtype=np.float64)
    inv = 10000.0 ** (-2.0 * i / 64.0)
    pos = np.arange(S, dtype=np.float64)
    ang = pos[None, :] * inv[:, None]
    cos = np.cos(ang)
    sin = np.sin(ang)
    cosT = np.empty((128, S), np.float32)
    sinT = np.empty((128, S), np.float32)
    for p_ in range(128):
        k = (p_ % 64) // 2
        cosT[p_] = cos[k]
        sinT[p_] = sin[k] if (p_ % 2) else -sin[k]
    perm = np.zeros((128, 128), np.float32)
    for r in range(0, 128, 2):
        perm[r + 1, r] = 1.0
        perm[r, r + 1] = 1.0
    tri = (np.arange(128)[:, None] <= np.arange(128)[None, :]).astype(np.float32)
    tri2 = np.concatenate([tri, tri], axis=1).astype(ml_dtypes.bfloat16)
    bsel = np.zeros((48, 8 * 128), np.float32)
    for base in (0, 32):
        for t in range(8):
            bsel[base + 2 * t, t * 128: t * 128 + 64] = 1.0
            bsel[base + 2 * t + 1, t * 128 + 64: (t + 1) * 128] = 1.0
    return dict(cosT=cosT, sinT=sinT,
                ident_bf=np.eye(128).astype(ml_dtypes.bfloat16),
                ident_f=np.eye(128).astype(np.float32),
                perm_bf=perm.astype(ml_dtypes.bfloat16), tri2=tri2,
                iota8=np.arange(8, dtype=np.float32)[None, :],
                bsel=bsel.astype(ml_dtypes.bfloat16))


STREAMS = ("tensor", "scalar", "vector", "gpsimd", "sync")
CSEM = {"tensor": "pe", "scalar": "act", "vector": "dve", "gpsimd": "pool", "sync": "syn"}
DSEM = {"scalar": "dsc", "gpsimd": "dgp", "sync": "dsy"}
NDG = {"sync": 8, "scalar": 8, "gpsimd": 4}
SEMS = ("pe", "act", "dve", "pool") + tuple(
    f"{DSEM[s]}{i}" for s in NDG for i in range(NDG[s]))


class Prog:

    def __init__(self):
        self.q = {s: [] for s in STREAMS}
        self.cnt = {}
        self.dma_n = {}
        self.dma_tok = {}
        self.last_w = {}
        self.readers = {}

    def add(self, stream, fn, r=(), w=(), deps=(), dma=False, hard=False):
        dep = set(d for d in deps if d is not None)
        if dma:
            base = DSEM[stream]
            ndg = NDG[stream]
            n = self.dma_n.get(stream, 0)
            self.dma_n[stream] = n + 1
            sem = f"{base}{n % ndg}"
            inc = 16
            if n >= ndg:
                dep.add(self.dma_tok[(stream, (n - ndg) % ndg)])
        else:
            sem = CSEM[stream]
            inc = 1
        cur = self.cnt.get(sem, 0) + inc
        self.cnt[sem] = cur
        tok = (sem, cur)
        if dma:
            self.dma_tok[(stream, (self.dma_n[stream] - 1) % NDG[stream])] = tok
        for res in r:
            if res in self.last_w:
                dep.add(self.last_w[res])
        for res in w:
            if res in self.last_w:
                dep.add(self.last_w[res])
            for t in self.readers.get(res, ()):
                dep.add(t)
        for res in r:
            self.readers.setdefault(res, []).append(tok)
        for res in w:
            self.last_w[res] = tok
            self.readers[res] = []
        dep.discard(tok)
        self.q[stream].append((sorted(dep), fn, sem, inc, hard))
        return tok

    def cur(self, res):
        toks = list(self.readers.get(res, ()))
        if res in self.last_w:
            toks.append(self.last_w[res])
        return toks

    def emit(self, nc, semh):
        totals = dict(self.cnt)
        with nc.Block() as block:
            for stream in STREAMS:
                tasks = self.q[stream]
                own = CSEM[stream]

                def body(eng, tasks=tasks, own=own, stream=stream):
                    guar = {}
                    nown = 0
                    for deps, fn, sem, inc, hard in tasks:
                        for (s2, v2) in deps:
                            if guar.get(s2, 0) < v2:
                                eng.wait_ge(semh[s2], v2)
                                guar[s2] = v2
                        fn(eng).then_inc(semh[sem], inc)
                        if sem == own:
                            nown += 1
                            guar[own] = max(guar.get(own, 0), 0)
                    if stream == "sync":
                        for s2 in SEMS:
                            if s2 in totals and guar.get(s2, 0) < totals[s2]:
                                eng.wait_ge(semh[s2], totals[s2])

                getattr(block, stream)(body)


def build_program(nb=NB, dbg=()):
    nc = bass.Bass("TRN2", target_bir_lowering=False, debug=False, num_swdge_queues=4)

    x_in = nc.dram_tensor("x", [nb, S, D], F32, kind="ExternalInput")
    W = {}
    for n, shp in [("Wq", [D, D]), ("bq", [D]), ("Wk", [D, D]), ("bk", [D]),
                   ("Wv", [D, D]), ("bv", [D]), ("Wo", [D, D]), ("bo", [D]),
                   ("Wp1", [D, NAV]), ("bp1", [NAV]), ("Wp2", [NAV, NEXP]), ("bp2", [NEXP]),
                   ("W1", [NEXP, D, FF]), ("b1", [NEXP, FF]), ("W2", [NEXP, FF, D]),
                   ("b2", [NEXP, D]), ("WoF", [D, D]), ("boF", [D])]:
        W[n] = nc.dram_tensor(n, shp, F32, kind="ExternalInput")
    C = {}
    for n, shp, dt in [("cosT", [128, S], F32), ("sinT", [128, S], F32),
                       ("ident_bf", [128, 128], BF16), ("ident_f", [128, 128], F32),
                       ("perm_bf", [128, 128], BF16), ("tri2", [128, 256], BF16),
                       ("bsel", [48, 8 * 128], BF16),
                       ("iota8", [1, 8], F32)]:
        C[n] = nc.dram_tensor(n, shp, dt, kind="ExternalInput")
    out_d = nc.dram_tensor("out", [nb, S, D], F32, kind="ExternalOutput")

    x1_scr = nc.dram_tensor("x1_scr", [S, D], F32, kind="Internal")
    xlt_scr = nc.dram_tensor("xlt_scr", [NCT, 128, S], BF16, kind="Internal")
    wt_scr = nc.dram_tensor("wt_scr", [4, 128, NCT * D], BF16, kind="Internal")
    wtF_scr = nc.dram_tensor("wtF_scr", [128, NCT * D], BF16, kind="Internal")

    dbg_out = {}
    for name, shp, dt in [
        ("d_xlnT", [128, NCT * S], BF16), ("d_QT", [128, NCT * S], BF16),
        ("d_KT", [128, NCT * S], BF16), ("d_Vt", [128, NST * 1040], BF16),
        ("d_ctx3", [128, NPAIR * CHUNK], BF16), ("d_sums3", [16, CHUNK], F32),
        ("d_beta3", [16, CHUNK], F32), ("d_ctxf3", [128, NPAIR * CHUNK], BF16),
        ("d_x1", [S, D], F32), ("d_row", [1, 8], F32), ("d_sel", [1, 1], I32),
        ("d_w1s", [128, NCT * FF], BF16), ("d_h0", [128, NH * CHUNK], BF16),
        ("d_xlt", [NCT, 128, S], BF16), ("d_lastln", [128, NCT], F32),
        ("d_b1c", [128, 16], F32), ("d_b2c", [128, NEXP], F32),
        ("d_xin0", [128, NCT * CHUNK], BF16), ("d_rs0", [128, NCT * CHUNK], BF16),
        ("d_hraw", [128, CHUNK], F32), ("d_b1g", [16, 128], F32),
        ("d_b1c_early", [128, 16], F32), ("d_r2", [128, NST], F32),
        ("d_m2r2", [128, NST], F32), ("d_cb", [128, 16], F32),
        ("d_pdsnap", [128, 16], F32), ("d_b1gsnap", [16, 128], F32),
        ("d_idf", [128, 128], F32), ("d_idb", [128, 128], BF16),
    ]:
        if name in dbg:
            dbg_out[name] = nc.dram_tensor(name, shp, dt, kind="ExternalOutput")

    off = [16384 + 128]

    def place(name, shape, dtype, at=None):
        esz = 2 if dtype == BF16 else 4
        nbytes = int(np.prod(shape[1:])) * esz
        if at is None:
            at = off[0]
            off[0] = at + ((nbytes + 31) // 32) * 32
        return nc.alloc_sbuf_tensor_at(name, shape, dtype, offset=at)

    R1 = place("R1", [128, NCT * S], BF16)
    R2 = place("R2", [128, NCT * S], BF16)
    R3 = place("R3", [128, NCT * S], BF16)
    R4 = place("R4", [128, NST * 1040], BF16)
    R5 = place("R5", [128, NCT * D], BF16)
    r6 = off[0]
    R6c = place("R6c", [128, NPAIR * CHUNK], BF16)
    R6p = place("R6p", [128, 2048], BF16)
    R6tab = place("R6tab", [128, S], F32, at=r6)
    r7 = off[0]
    R7f = place("R7f", [128, NPAIR * CHUNK], BF16)
    R7tab = place("R7tab", [128, S], F32, at=r7)
    off[0] = max(off[0], r7 + 4 * S)
    R8a = place("R8a", [128, D], F32)
    R8b = place("R8b", [128, D], F32)
    R9 = place("R9", [128, 2 * S], BF16)
    R10a = place("R10a", [128, D], F32)
    R10b = place("R10b", [128, D], F32)
    X1L = place("X1L", [128, NCT * CHUNK], BF16)
    x1l_at = X1L.manual_sbuf_range[0] if X1L.manual_sbuf_range else 0
    SMb1g = nc.alloc_sbuf_tensor_at("SM_b1g", [16, 128], F32, offset=x1l_at + 2048)
    SMb2g = nc.alloc_sbuf_tensor_at("SM_b2g", [NEXP, 128], F32, offset=x1l_at + 2560)
    SMbvrow = nc.alloc_sbuf_tensor_at("SM_bvrow", [1, D], BF16, offset=X1L.manual_sbuf_range[0]
                                      if X1L.manual_sbuf_range else 0)

    SM = {}
    for n, shp, dt in [
        ("r2", [128, NST], F32), ("m2r2", [128, NST], F32), ("r2p1", [128, NST], F32),
        ("stats", [128, 24], F32), ("mv", [128, 4], F32), ("ln1", [128, 6], F32),
        ("bqc", [128, NCT], F32), ("bkc", [128, NCT], F32), ("boc", [128, NCT], F32),
        ("boFc", [128, NCT], F32), ("b2c", [128, NCT], F32),
        ("cb0", [128, 8], F32), ("b1c", [128, 16], F32), ("cb1", [128, 8], F32),
        ("bp1", [128, 1], F32), ("wp2", [128, NEXP], F32), ("bp2", [NEXP, 1], F32),
        ("tri", [128, 256], BF16), ("identb", [128, 128], BF16),
        ("identf", [128, 128], F32), ("perm", [128, 128], BF16), ("iota8", [1, 8], F32),
        ("ones1b", [1, 128], BF16), ("ones1f", [1, 128], F32),
        ("row", [1, 8], F32), ("row2", [1, 8], F32), ("row3", [1, 8], F32),
        ("m", [1, 1], F32), ("umin", [1, 1], F32), ("self", [1, 1], F32), ("seli", [1, 1], I32),
        ("g", [128, 1], F32), ("lastT", [128, NCT], F32), ("lastln", [128, NCT], F32),
        ("epsln", [128, 1], F32), ("epssm", [128, 1], F32),
        ("bsel", [48, 8 * 128], BF16), ("bhi", [16, CHUNK], BF16),
        ("sc1", [1, 2], F32), ("bc1", [128, 2], F32),
        ("logc", [NEXP, 1], F32),
        ("xln2", [128, D], BF16), ("xln2b", [128, D], BF16),
    ]:
        SM[n] = place("SM_" + n, shp, dt)
    assert off[0] <= 16384 + 212576 - 64, f"sbuf overflow: {off[0]}"

    PA = nc.alloc_psum_tensor("PA", [128, 1024], F32)
    PB = nc.alloc_psum_tensor("PB", [128, 1024], F32)
    PC = nc.alloc_psum_tensor("PC", [128, 1024], F32)
    PD = nc.alloc_psum_tensor("PD", [128, 512], F32)
    PE2 = nc.alloc_psum_tensor("PE2", [128, 512], F32)
    PDb = PD.bitcast(BF16)
    PE2b = PE2.bitcast(BF16)

    p = Prog()
    from contextlib import ExitStack
    stack = ExitStack()
    semh = {s: stack.enter_context(nc.semaphore(s)) for s in SEMS}
    greg1 = stack.enter_context(nc.gpsimd.register("greg1"))

    p.add("sync", lambda e: e.dma_start(SM["tri"][:, :], C["tri2"][:, :]), w=["tri"], dma=True)
    p.add("sync", lambda e: e.dma_start(SM["identb"][:, :], C["ident_bf"][:, :]), w=["identb"], dma=True)
    p.add("sync", lambda e: e.dma_start(SM["identf"][:, :], C["ident_f"][:, :]), w=["identf"], dma=True)
    p.add("sync", lambda e: e.dma_start(SM["perm"][:, :], C["perm_bf"][:, :]), w=["perm"], dma=True)
    p.add("sync", lambda e: e.dma_start(SM["iota8"][:, :], C["iota8"][:, :]), w=["iota8"], dma=True)
    p.add("sync", lambda e: e.dma_start(SM["bsel"][:, :], C["bsel"][:, :]), w=["bsel"], dma=True)
    p.add("sync", lambda e: e.dma_start(
        SM["bp1"][:, :], W["bp1"].ap().rearrange("(p o) -> p o", o=1)), w=["bp1"], dma=True)
    p.add("sync", lambda e: e.dma_start(SM["wp2"][:, :], W["Wp2"][:, :]), w=["wp2"], dma=True)
    p.add("sync", lambda e: e.dma_start(
        SM["bp2"][:, :], W["bp2"].ap().rearrange("(p o) -> p o", o=1)), w=["bp2"], dma=True)
    p.add("vector", lambda e: e.memset(SM["ones1b"][:, :], 1.0), w=["ones1b"])
    p.add("vector", lambda e: e.memset(SM["epsln"][:, :], EPS_LN), w=["epsln"])
    p.add("vector", lambda e: e.memset(SM["epssm"][:, :], EPS_SM), w=["epssm"])
    p.add("vector", lambda e: e.memset(SM["ones1f"][:, :], 1.0), w=["ones1f"])

    for i, wkey in enumerate(("Wq", "Wk", "Wv", "Wo")):
        p.add("gpsimd", lambda e, wkey=wkey: e.dma_start(
            R5[:, :].rearrange("q (dt c) -> q dt c", dt=NCT),
            W[wkey].ap().rearrange("(dt q) c -> q dt c", q=128)),
            w=["R5"], dma=True)
        for ct in range(NCT):
            pb = PDb if ct % 2 == 0 else PE2b
            pk = "PD" if ct % 2 == 0 else "PE2"

            def tf(e, ct=ct, pb=pb):
                for dt in range(NCT):
                    m = e.transpose(pb[:, dt * 128:(dt + 1) * 128],
                                    R5[:, dt * D + ct * 128: dt * D + (ct + 1) * 128],
                                    SM["identb"][:, :])
                return m
            p.add("tensor", tf, r=["R5", "identb"], w=[pk])
            p.add("scalar", lambda e, ct=ct, pb=pb: e.copy(
                R3[:, ct * D:(ct + 1) * D], pb[:, 0:1024]), r=[pk], w=["R3"])
        p.add("scalar", lambda e, i=i: e.dma_start(wt_scr[i, :, :], R3[:, 0:NCT * D]),
              r=["R3"], w=[f"wt{i}"], dma=True)

    R5f32 = R5.bitcast(F32)
    for hh in range(2):
        p.add("sync", lambda e, hh=hh: e.dma_start(
            R5f32[:, 0:4096].rearrange("q (ht ee) -> q ht ee", ht=4),
            W["WoF"].ap().rearrange("(ht q) ee -> q ht ee", q=128)[:, hh * 4:hh * 4 + 4, :]),
            w=["R5"], dma=True)
        p.add("vector", lambda e: e.tensor_copy(R3[:, 0:4096], R5f32[:, 0:4096]),
              r=["R5"], w=["R3"])
        p.add("scalar", lambda e, hh=hh: e.dma_start(
            wtF_scr[:, hh * 4096:(hh + 1) * 4096], R3[:, 0:4096]),
            r=["R3"], w=["wtF"], dma=True)

    for bkey, dstk in (("bq", "bqc"), ("bk", "bkc"), ("bo", "boc"), ("boF", "boFc")):
        p.add("sync", lambda e, bkey=bkey: e.dma_start(
            R10a[0:1, 0:1024], W[bkey].ap().rearrange("(o d) -> o d", o=1)),
            w=["R10a"], dma=True)

        def tf(e):
            for ct in range(NCT):
                m = e.transpose(PD[:, ct:ct + 1], R10a[0:1, ct * 128:(ct + 1) * 128],
                                SM["identf"][0:1, 0:1])
            return m
        p.add("tensor", tf, r=["R10a", "identf"], w=["PD"])
        p.add("scalar", lambda e, dstk=dstk: e.copy(SM[dstk][:, 0:NCT], PD[:, 0:NCT]),
              r=["PD"], w=[dstk])

    for b in range(nb):
        for st in range(NST):
            buf, bk_ = (R8a, "R8a") if st % 2 == 0 else (R8b, "R8b")
            bkq = [bk_ + ("L", "L2", "H", "H2")[i] for i in range(4)]
            for qi in range(4):
                eng = "sync"
                p.add(eng, lambda e, b=b, st=st, buf=buf, qi=qi: e.dma_start(
                    buf[qi * 32:(qi + 1) * 32, :],
                    x_in[b, st * 128 + qi * 32: st * 128 + (qi + 1) * 32, :]),
                    w=[bkq[qi]], dma=True)

            q_ = st % 2
            XL = SM["xln2"] if q_ == 0 else SM["xln2b"]
            sq, mq, lq = 12 * q_, 2 * q_, 3 * q_
            rs = lambda n: n + str(q_)

            def stf(e, buf=buf, sq=sq):
                e.bn_stats(SM["stats"][:, sq:sq + 6], buf[:, 0:512])
                return e.bn_stats(SM["stats"][:, sq + 6:sq + 12], buf[:, 512:1024])
            p.add("vector", stf, r=bkq, w=[rs("stats")])
            p.add("vector", lambda e, sq=sq, mq=mq: e.bn_aggr(
                SM["mv"][:, mq:mq + 2], SM["stats"][:, sq:sq + 12]),
                r=[rs("stats")], w=[rs("mv")], hard=True)
            p.add("scalar", lambda e, mq=mq, lq=lq: e.activation(
                SM["ln1"][:, lq + 2:lq + 3], SM["mv"][:, mq + 1:mq + 2], AF.Ln,
                bias=SM["epsln"][:, 0:1]),
                r=[rs("mv"), "epsln"], w=[rs("ln1x")], hard=True)
            p.add("scalar", lambda e, lq=lq: e.activation(
                SM["ln1"][:, lq:lq + 1], SM["ln1"][:, lq + 2:lq + 3], AF.Exp, scale=-0.5),
                r=[rs("ln1x")], w=[rs("ln1")], hard=True)
            p.add("vector", lambda e, mq=mq, lq=lq: e.scalar_tensor_tensor(
                SM["ln1"][:, lq + 1:lq + 2], SM["mv"][:, mq:mq + 1], -1.0,
                SM["ln1"][:, lq:lq + 1],
                ALU.mult, ALU.mult), r=[rs("mv"), rs("ln1")], w=[rs("ln1b")], hard=True)
            p.add("scalar", lambda e, buf=buf, lq=lq, XL=XL: e.activation(
                XL[:, :], buf[:, :], AF.Identity,
                bias=SM["ln1"][:, lq + 1:lq + 2], scale=SM["ln1"][:, lq:lq + 1]),
                r=bkq + [rs("ln1"), rs("ln1b")], w=[rs("xln2")], hard=True)
            pb, pk = (PDb, "PD") if st % 2 == 0 else (PE2b, "PE2")

            def tf(e, pb=pb, XL=XL):
                for ct in range(NCT):
                    m = e.transpose(pb[:, ct * 128:(ct + 1) * 128],
                                    XL[:, ct * 128:(ct + 1) * 128], SM["identb"][:, :])
                return m
            p.add("tensor", tf, r=[rs("xln2"), "identb"], w=[pk])
            p.add("vector", lambda e, st=st, pb=pb: e.tensor_copy(
                R1[:, :].rearrange("q (ct s) -> q ct s", ct=NCT)[:, :, st * 128:(st + 1) * 128],
                pb[:, 0:1024].rearrange("q (ct s) -> q ct s", ct=NCT)),
                r=[pk], w=["R1h" + str(st // 8)])

        if b == 0 and "d_xlnT" in dbg_out:
            p.add("sync", lambda e: e.dma_start(dbg_out["d_xlnT"][:, :], R1[:, :]),
                  r=["R1h0", "R1h1"], dma=True)

        p.add("sync", lambda e: e.dma_start(R6tab[:, :], C["cosT"][:, :]),
              w=["R6tab", "R6c"], dma=True)
        p.add("sync", lambda e: e.dma_start(R7tab[:, :], C["sinT"][:, :]),
              w=["R7tab", "R7f"], dma=True)

        def proj_rope(Wi, dstT, dstks, bck):
            p.add("sync", lambda e, Wi=Wi: e.dma_start(R5[:, :], wt_scr[Wi, :, :]),
                  r=[f"wt{Wi}"], w=["R5"], dma=True)

            def emit_perm_rope(unit):
                dt, qq = unit // 2, unit % 2
                o = 1024 * (unit % 2)
                rk = "R9q" + str(unit % 2)
                if unit % 2 == 0:
                    pd_ap, pe_ap, pdk, pek = PD[:, 0:512], PE2[:, 0:512], "PD", "PE2"
                else:
                    pd_ap, pe_ap, pdk, pek = PC[:, 0:512], PC[:, 512:1024], "PC0", "PC1"

                def permf(e, o=o, pd_ap=pd_ap, pe_ap=pe_ap):
                    e.matmul(pd_ap, SM["perm"][:, :], R9[:, o:o + 512],
                             start=True, stop=True)
                    return e.matmul(pe_ap, SM["perm"][:, :], R9[:, o + 512:o + 1024],
                                    start=True, stop=True)
                p.add("tensor", permf, r=[rk, "perm"], w=[pdk, pek])
                scol = qq * 1024

                def ropef1(e, o=o, scol=scol, pd_ap=pd_ap, pe_ap=pe_ap):
                    e.tensor_mul(R10a[:, 0:1024], R9[:, o:o + 1024],
                                 R6tab[:, scol:scol + 1024])
                    e.tensor_mul(R10b[:, 0:512], pd_ap, R7tab[:, scol:scol + 512])
                    return e.tensor_mul(R10b[:, 512:1024], pe_ap,
                                        R7tab[:, scol + 512:scol + 1024])
                p.add("vector", ropef1, r=[rk, pdk, pek, "R6tab", "R7tab"],
                      w=["R10a", "R10b", "R10bs0", "R10bs1", "R10bln0", "R10bln1",
                         "bhi", "blo"])
                p.add("vector", lambda e, dt=dt, scol=scol: e.tensor_add(
                    dstT[:, dt * S + scol: dt * S + scol + 1024],
                    R10a[:, 0:1024], R10b[:, 0:1024]),
                    r=["R10a", "R10b"], w=dstks(dt), hard=True)

            for dt in range(NCT):
                for qq in range(2):
                    unit = dt * 2 + qq
                    pb, pk = (PA, "PA") if unit % 2 == 0 else (PB, "PB")

                    def mf(e, dt=dt, qq=qq, pb=pb):
                        for ct in range(NCT):
                            for nn in range(2):
                                m = e.matmul(
                                    pb[:, nn * 512:(nn + 1) * 512],
                                    R5[:, ct * D + dt * 128: ct * D + (dt + 1) * 128],
                                    R1[:, ct * S + qq * 1024 + nn * 512:
                                       ct * S + qq * 1024 + (nn + 1) * 512],
                                    start=(ct == 0), stop=(ct == NCT - 1))
                        return m
                    p.add("tensor", mf, r=["R5", "R1h" + str(qq)], w=[pk + "0", pk + "1", pk])
                    o = 1024 * (unit % 2)
                    rk = "R9q" + str(unit % 2)
                    p.add("scalar", lambda e, pb=pb, dt=dt, o=o, bck=bck: e.activation(
                        R9[:, o:o + 1024], pb[:, 0:1024], AF.Identity,
                        bias=SM[bck][:, dt:dt + 1]),
                        r=[pk, pk + "0", pk + "1", bck], w=[rk])
                    if unit >= 1:
                        emit_perm_rope(unit - 1)
            emit_perm_rope(15)

        proj_rope(0, R2, lambda dt: ["R2p%d" % dt], "bqc")
        proj_rope(1, R3,
                  lambda dt: ["R3", "R3h0", "R3h1", "R3h2", "R3h3", "R3r"], "bkc")

        p.add("vector", lambda e: e.memset(
            R4[:, :].rearrange("q (t h c) -> q t h c", t=NST, h=NH)[:, :, :, 64:65], 1.0),
            w=["R4"])
        p.add("gpsimd", lambda e: e.dma_start(
            SMbvrow[:, :], W["bv"].ap().rearrange("(o d) -> o d", o=1)), w=["X1L"], dma=True)
        p.add("sync", lambda e: e.dma_start(R5[:, :], wt_scr[2, :, :]),
              r=["wt2"], w=["R5"], dma=True)
        for st in range(NST):
            pbv, pk = (PA, "PA") if st % 2 == 0 else (PB, "PB")

            def vf(e, st=st, pbv=pbv):
                for ct in range(NCT):
                    for nn in range(2):
                        e.matmul(pbv[:, nn * 512:(nn + 1) * 512],
                                 R1[:, ct * S + st * 128: ct * S + (st + 1) * 128],
                                 R5[:, ct * D + nn * 512: ct * D + (nn + 1) * 512],
                                 start=(ct == 0), stop=False)
                for nn in range(2):
                    m = e.matmul(pbv[:, nn * 512:(nn + 1) * 512], SM["ones1b"][:, :],
                                 SMbvrow[:, nn * 512:(nn + 1) * 512], start=False, stop=True)
                return m
            p.add("tensor", vf, r=["R5", "R1h" + str(st // 8), "ones1b", "X1L"],
                  w=[pk, pk + "0", pk + "1"])
            p.add("vector", lambda e, st=st, pbv=pbv: e.tensor_copy(
                R4[:, :].rearrange("q (t h c) -> q t h c", t=NST, h=NH)[:, st, :, 0:64],
                pbv[:, 0:1024].rearrange("q (h c) -> q h c", h=NH)),
                r=[pk, pk + "0", pk + "1"], w=["R4"])

        if b == 0 and "d_QT" in dbg_out:
            p.add("sync", lambda e: e.dma_start(dbg_out["d_QT"][:, :], R2[:, :]),
                  r=["R2p%d" % i for i in range(NPAIR)], dma=True)
        if b == 0 and "d_KT" in dbg_out:
            p.add("sync", lambda e: e.dma_start(dbg_out["d_KT"][:, :], R3[:, :]), r=["R3"], dma=True)
        if b == 0 and "d_Vt" in dbg_out:
            p.add("sync", lambda e: e.dma_start(dbg_out["d_Vt"][:, :], R4[:, :]), r=["R4"], dma=True)

        p.add("sync", lambda e: e.dma_start(R5[:, :], wt_scr[3, :, :]),
              r=["wt3"], w=["R5"], dma=True)

        R10bb = R10b.bitcast(BF16)
        tail_tt = {}
        tail_beta = [None]

        def emit_attn(c, hooks=None, mid_hooks=None):
            pi = 32 * (c % 2)
            for t in range(NPAIR):
                njs = 4 * c + 4

                def sc_mm(e, t, j, qoff, w_, stb):
                    kb = t * S + j * 128
                    qb = t * S + c * CHUNK + qoff
                    e.matmul(stb[:, qoff:CHUNK], R3[0:64, kb:kb + 128],
                             R2[0:64, qb:qb + w_], start=True, stop=True,
                             tile_position=(0, 0))
                    return e.matmul(stb[:, CHUNK + qoff:2 * CHUNK], R3[64:128, kb:kb + 128],
                                    R2[64:128, qb:qb + w_], start=True, stop=True,
                                    tile_position=(64, 0))

                def pv_mm(e, t, j, qoff, w_, po, first, lastj):
                    vb = j * 1040 + (2 * t) * 65
                    e.matmul(PC[0:65, qoff:CHUNK], R4[:, vb:vb + 65],
                             R6p[:, po + qoff:po + qoff + w_], start=first, stop=lastj)
                    return e.matmul(PC[0:65, CHUNK + qoff:2 * CHUNK],
                                    R4[:, vb + 65:vb + 130],
                                    R6p[:, po + CHUNK + qoff:po + CHUNK + qoff + w_],
                                    start=first, stop=lastj)

                def piece(j):
                    qoff = max(0, 128 * j - CHUNK * c)
                    return qoff, CHUNK - qoff, (PA, "PA") if j % 2 == 0 else (PB, "PB")

                for j in (0, 1):
                    qoff, w_, (stb, sk) = piece(j)
                    p.add("tensor",
                          lambda e, t=t, j=j, qoff=qoff, w_=w_, stb=stb:
                          sc_mm(e, t, j, qoff, w_, stb),
                          r=["R2p%d" % t, "R3"], w=[sk + "0", sk + "1", sk])
                for j in range(njs):
                    qoff, w_, (stb, sk) = piece(j)
                    po = 1024 * (j % 2)
                    pkk = "R6p" + str(j % 2)

                    def ef(e, qoff=qoff, stb=stb, po=po):
                        if qoff == 0:
                            return e.activation(R6p[:, po:po + 1024], stb[:, 0:1024],
                                                AF.Exp, scale=0.125)
                        src_ = stb[:, 0:1024].rearrange("q (bb z) -> q bb z", bb=2)[:, :, qoff:CHUNK]
                        dst = R6p[:, po:po + 1024].rearrange(
                            "q (bb z) -> q bb z", bb=2)[:, :, qoff:CHUNK]
                        return e.activation(dst, src_, AF.Exp, scale=0.125)
                    p.add("scalar", ef, r=[sk, sk + "0", sk + "1"], w=[pkk])
                    if 128 * j >= CHUNK * c:
                        def mkf(e, qoff=qoff, po=po):
                            dst = R6p[:, po:po + 1024].rearrange(
                                "q (bb z) -> q bb z", bb=2)[:, :, qoff:qoff + 128]
                            src_ = SM["tri"][:, :].rearrange("q (bb z) -> q bb z", bb=2)
                            return e.tensor_tensor(dst, dst, src_, ALU.mult)
                        p.add("vector", mkf, r=["tri", pkk], w=[pkk])
                    jn = j + 2
                    if FUSE_PV and jn < njs:
                        qoff2, w2_, (stb2, sk2_) = piece(jn)

                        def fused(e, t=t, j=j, jn=jn, qoff=qoff, w_=w_, po=po,
                                  first=(j == 0), lastj=(j == njs - 1),
                                  qoff2=qoff2, w2_=w2_, stb2=stb2):
                            pv_mm(e, t, j, qoff, w_, po, first, lastj)
                            return sc_mm(e, t, jn, qoff2, w2_, stb2)
                        p.add("tensor", fused, r=[pkk, "R4", "R2p%d" % t, "R3"],
                              w=["PC0", "PC1", sk2_ + "0", sk2_ + "1", sk2_])
                    else:
                        if not FUSE_PV and jn < njs:
                            qoff2, w2_, (stb2, sk2_) = piece(jn)
                            p.add("tensor",
                                  lambda e, jn=jn, qoff2=qoff2, w2_=w2_, stb2=stb2, t=t:
                                  sc_mm(e, t, jn, qoff2, w2_, stb2),
                                  r=["R2p%d" % t, "R3"], w=[sk2_ + "0", sk2_ + "1", sk2_])
                        p.add("tensor",
                              lambda e, t=t, j=j, qoff=qoff, w_=w_, po=po,
                              first=(j == 0), lastj=(j == njs - 1):
                              pv_mm(e, t, j, qoff, w_, po, first, lastj),
                              r=[pkk, "R4"], w=["PC0", "PC1"])
                if mid_hooks and t in mid_hooks:
                    mid_hooks[t]()
                dep_tt = tail_tt.get(t)
                p.add("vector", lambda e, t=t: e.tensor_copy(
                    R6c[0:64, t * CHUNK:(t + 1) * CHUNK], PC[0:64, 0:CHUNK]),
                    r=["PC0"], w=["R6c", "R6tab"], deps=[dep_tt])
                p.add("vector", lambda e, t=t: e.tensor_copy(
                    R6c[64:128, t * CHUNK:(t + 1) * CHUNK], PC[0:64, CHUNK:2 * CHUNK]),
                    r=["PC1"], w=["R6c"], deps=[dep_tt])
                p.add("vector", lambda e: e.tensor_copy(
                    R10b[64:65, 0:2 * CHUNK], PC[64:65, 0:2 * CHUNK]),
                    r=["PC0", "PC1", "R10b"], w=["R7tA"])
                p.add("sync", lambda e, t=t, pi=pi: e.dma_start(
                    R10b[pi + 2 * t:pi + 2 * t + 1, 0:CHUNK], R10b[64:65, 0:CHUNK]),
                    r=["R7tA", "R10b"], w=["R10bs" + str(c % 2)],
                    deps=[tail_beta[0]], dma=True)
                p.add("sync", lambda e, t=t, pi=pi: e.dma_start(
                    R10b[pi + 2 * t + 1:pi + 2 * t + 2, 0:CHUNK],
                    R10b[64:65, CHUNK:2 * CHUNK]),
                    r=["R7tA", "R10b"], w=["R10bs" + str(c % 2)],
                    deps=[tail_beta[0]], dma=True)
                if hooks and t in hooks:
                    hooks[t]()

        def emit_tailA_beta(c):
            pi = 32 * (c % 2)
            sk2 = "R10bs" + str(c % 2)
            kln = "R10bln" + str(c % 2)
            BHI = R10bb[pi:pi + 16, 1024:1024 + CHUNK]
            BLO = R10bb[pi:pi + 16, 1024 + CHUNK:1024 + 2 * CHUNK]
            p.add("scalar", lambda e, pi=pi: e.activation(
                R10b[pi:pi + 16, CHUNK:2 * CHUNK], R10b[pi:pi + 16, 0:CHUNK], AF.Ln,
                bias=SM["epssm"][0:16, 0:1]),
                r=[sk2, "epssm"], w=[kln], hard=True)
            p.add("scalar", lambda e, pi=pi: e.activation(
                R10b[pi:pi + 16, 0:CHUNK], R10b[pi:pi + 16, CHUNK:2 * CHUNK],
                AF.Exp, scale=-1.0),
                r=[kln], w=[sk2], hard=True)

            if b == 0 and c == 3:
                if "d_ctx3" in dbg_out:
                    p.add("sync", lambda e: e.dma_start(dbg_out["d_ctx3"][:, :], R6c[:, :]),
                          r=["R6c"], dma=True)
                if "d_beta3" in dbg_out:
                    p.add("sync", lambda e, pi=pi: e.dma_start(
                        dbg_out["d_beta3"][:, :], R10b[pi:pi + 16, 0:CHUNK]),
                        r=[sk2], dma=True)

            p.add("scalar", lambda e, pi=pi, BHI=BHI: e.activation(
                BHI, R10b[pi:pi + 16, 0:CHUNK], AF.Identity),
                r=[sk2], w=["bhi", kln], hard=True)
            tail_beta[0] = p.add("vector", lambda e, pi=pi, BHI=BHI, BLO=BLO: e.tensor_sub(
                BLO, R10b[pi:pi + 16, 0:CHUNK], BHI),
                r=[sk2, "bhi"], w=["blo", kln], hard=True)
            return pi, BHI, BLO

        def emit_tailA_bcf(c, pi, BHI, BLO, t):
            pbc, pck = (PD, "PD") if t % 2 == 0 else (PE2, "PE2")

            def bcf(e, t=t, pbc=pbc, pi=pi, BHI=BHI, BLO=BLO):
                e.matmul(pbc[:, 0:CHUNK], SM["bsel"][pi:pi + 16, t * 128:(t + 1) * 128],
                         BHI, start=True, stop=False)
                return e.matmul(pbc[:, 0:CHUNK], SM["bsel"][pi:pi + 16, t * 128:(t + 1) * 128],
                                BLO, start=False, stop=True)
            p.add("tensor", bcf, r=["bhi", "blo", "bsel"], w=[pck])
            tail_tt[t] = p.add("vector", lambda e, t=t, pbc=pbc, pck=pck: e.tensor_tensor(
                R7f[:, t * CHUNK:(t + 1) * CHUNK],
                R6c[:, t * CHUNK:(t + 1) * CHUNK],
                pbc[:, 0:CHUNK], ALU.mult),
                r=["R6c", pck], w=["R7f", "R7tab"])

        def emit_tailA(c):
            pi, BHI, BLO = emit_tailA_beta(c)
            for t in range(NPAIR):
                emit_tailA_bcf(c, pi, BHI, BLO, t)

            if b == 0 and c == 3 and "d_ctxf3" in dbg_out:
                p.add("sync", lambda e: e.dma_start(dbg_out["d_ctxf3"][:, :], R7f[:, :]),
                      r=["R7f"], dma=True)

        def emit_tailB(c):
            for et in range(NCT):
                pp, pk = (PD, "PD") if et % 2 == 0 else (PE2, "PE2")

                def opf(e, et=et, pp=pp):
                    for pt in range(NCT):
                        m = e.matmul(pp[:, 0:512],
                                     R5[:, pt * D + et * 128:pt * D + (et + 1) * 128],
                                     R7f[:, pt * CHUNK:(pt + 1) * CHUNK],
                                     start=(pt == 0), stop=(pt == NCT - 1))
                    return m
                p.add("tensor", opf, r=["R5", "R7f"], w=[pk])
                p.add("vector", lambda e, et=et, pp=pp: e.tensor_scalar_add(
                    R9[:, et * 512:(et + 1) * 512], pp[:, 0:512],
                    SM["boc"][:, et:et + 1]),
                    r=[pk, "boc"], w=["R9q" + str(et // 2)])

            for u in range(4):
                st = 4 * c + u
                pbj, pjk = (PDb, "PD") if u % 2 == 0 else (PE2b, "PE2")

                def trf(e, u=u, pbj=pbj):
                    for et in range(NCT):
                        m = e.transpose(pbj[:, et * 128:(et + 1) * 128],
                                        R9[:, et * 512 + u * 128: et * 512 + (u + 1) * 128],
                                        SM["identb"][:, :])
                    return m
                p.add("tensor", trf, r=["R9q0", "R9q1", "R9q2", "R9q3", "identb"], w=[pjk])
                for qi in range(4):
                    eng = ("sync", "scalar", "sync", "scalar")[qi]
                    res = ("R8aL", "R8aL2", "R8aH", "R8aH2")[qi]
                    p.add(eng, lambda e, b=b, c=c, u=u, qi=qi: e.dma_start(
                        R8a[qi * 32:(qi + 1) * 32, :],
                        x_in[b, c * CHUNK + u * 128 + qi * 32:
                             c * CHUNK + u * 128 + (qi + 1) * 32, :]),
                        w=[res], dma=True)
                p.add("vector", lambda e, pbj=pbj: e.tensor_tensor(
                    R8b[:, :], R8a[:, :], pbj[:, 0:1024], ALU.add),
                    r=["R8aL", "R8aL2", "R8aH", "R8aH2"] + [pjk], w=["R8bL", "R8bL2", "R8bH", "R8bH2"])
                p.add("scalar", lambda e, c=c, u=u: e.dma_start(
                    x1_scr[c * CHUNK + u * 128: c * CHUNK + (u + 1) * 128, :], R8b[:, :]),
                    r=["R8bL", "R8bL2", "R8bH", "R8bH2"], w=["x1s" + str(st)], dma=True)

                def stf2(e):
                    e.bn_stats(SM["stats"][:, 0:6], R8b[:, 0:512])
                    return e.bn_stats(SM["stats"][:, 6:12], R8b[:, 512:1024])
                p.add("vector", stf2, r=["R8bL", "R8bL2", "R8bH", "R8bH2"], w=["stats0"])
                p.add("vector", lambda e: e.bn_aggr(SM["mv"][:, 0:2], SM["stats"][:, 0:12]),
                      r=["stats0"], w=["mv0"], hard=True)
                p.add("scalar", lambda e: e.activation(
                    SM["ln1"][:, 2:3], SM["mv"][:, 1:2], AF.Ln, bias=SM["epsln"][:, 0:1]),
                    r=["mv0", "epsln"], w=["ln1x0"], hard=True)
                p.add("scalar", lambda e, st=st: e.activation(
                    SM["r2"][:, st:st + 1], SM["ln1"][:, 2:3], AF.Exp, scale=-0.5),
                    r=["ln1x0"], w=["r2"], hard=True)
                p.add("vector", lambda e, st=st: e.scalar_tensor_tensor(
                    SM["m2r2"][:, st:st + 1], SM["mv"][:, 0:1], -1.0, SM["r2"][:, st:st + 1],
                    ALU.mult, ALU.mult), r=["mv0", "r2"], w=["m2r2"], hard=True)
                p.add("scalar", lambda e, st=st: e.activation(
                    SM["xln2"][:, :], R8b[:, :], AF.Identity,
                    bias=SM["m2r2"][:, st:st + 1], scale=SM["r2"][:, st:st + 1]),
                    r=["R8bL", "R8bL2", "R8bH", "R8bH2"] + ["r2", "m2r2"],
                    w=["xln20"], hard=True)
                pbx, pxk = (PE2b, "PE2") if u % 2 == 0 else (PDb, "PD")

                def txf(e, pbx=pbx):
                    for ct in range(NCT):
                        m = e.transpose(pbx[:, ct * 128:(ct + 1) * 128],
                                        SM["xln2"][:, ct * 128:(ct + 1) * 128],
                                        SM["identb"][:, :])
                    return m
                p.add("tensor", txf, r=["xln20", "identb"], w=[pxk])
                p.add("vector", lambda e, u=u, pbx=pbx: e.tensor_copy(
                    X1L[:, :].rearrange("q (ct z) -> q ct z", ct=NCT)[:, :, u * 128:(u + 1) * 128],
                    pbx[:, 0:1024].rearrange("q (ct z) -> q ct z", ct=NCT)),
                    r=[pxk], w=["X1L"])
                if c == 3 and u == 3:
                    p.add("sync", lambda e: e.dma_start(R10a[0:1, 0:1024], R8b[127:128, :]),
                          r=["R8bL", "R8bL2", "R8bH", "R8bH2"], w=["R10a"], dma=True)
            p.add("vector", lambda e, c=c: e.tensor_scalar_add(
                SM["r2p1"][:, 4 * c:4 * c + 4], SM["r2"][:, 4 * c:4 * c + 4], 1.0),
                r=["r2"], w=["r2p1"], hard=True)
            p.add("scalar", lambda e, c=c: e.dma_start(
                xlt_scr[:, :, c * CHUNK:(c + 1) * CHUNK].rearrange("ct q z -> q ct z"),
                X1L[:, :].rearrange("q (ct z) -> q ct z", ct=NCT)),
                r=["X1L"], w=["xlt" + str(c)], dma=True)

        def router_stages():
          def s1():
            def rtr_t1(e):
                for ct in range(NCT):
                    m = e.transpose(PD[:, ct:ct + 1], R10a[0:1, ct * 128:(ct + 1) * 128],
                                    SM["identf"][0:1, 0:1])
                return m
            p.add("tensor", rtr_t1, r=["R10a", "identf"], w=["PD"])
            p.add("scalar", lambda e: e.copy(SM["lastT"][:, :], PD[:, 0:NCT]),
                  r=["PD"], w=["lastT"], hard=True)
            p.add("sync", lambda e: e.dma_start(SM["sc1"][0:1, 0:1], SM["r2"][127:128, 15:16]),
                  r=["r2"], w=["sc1"], dma=True)
            p.add("sync", lambda e: e.dma_start(SM["sc1"][0:1, 1:2], SM["m2r2"][127:128, 15:16]),
                  r=["m2r2"], w=["sc1"], dma=True)
            p.add("tensor", lambda e: e.matmul(
                PE2[:, 0:2], SM["ones1f"][:, :], SM["sc1"][0:1, 0:2], start=True, stop=True),
                r=["sc1", "ones1f"], w=["PE2"])
            p.add("scalar", lambda e: e.copy(SM["bc1"][:, :], PE2[:, 0:2]),
                  r=["PE2"], w=["bc1"], hard=True)
            p.add("scalar", lambda e: e.activation(
                SM["lastln"][:, :], SM["lastT"][:, :], AF.Identity,
                bias=SM["bc1"][:, 1:2], scale=SM["bc1"][:, 0:1]),
                r=["lastT", "bc1"], w=["lastln"], hard=True)
            if b == 0 and "d_lastln" in dbg_out:
                p.add("sync", lambda e: e.dma_start(dbg_out["d_lastln"][:, :], SM["lastln"][:, :]),
                      r=["lastln"], dma=True)
            p.add("sync", lambda e: e.dma_start(
                R10a[:, :].rearrange("q (ct n) -> q ct n", ct=NCT),
                W["Wp1"].ap().rearrange("(ct q) n -> q ct n", q=128)),
                w=["R10a"], dma=True)

          def s2():
            def rtr_mm(e):
                for ct in range(NCT):
                    m = e.matmul(PD[:, 0:1], R10a[:, ct * 128:(ct + 1) * 128],
                                 SM["lastln"][:, ct:ct + 1], start=(ct == 0),
                                 stop=(ct == NCT - 1))
                return m
            p.add("tensor", rtr_mm, r=["R10a", "lastln"], w=["PD"])
            p.add("scalar", lambda e: e.activation(
                SM["g"][:, :], PD[:, 0:1], AF.Gelu, bias=SM["bp1"][:, 0:1]),
                r=["PD", "bp1"], w=["g"], hard=True)
            p.add("tensor", lambda e: e.matmul(
                PE2[0:NEXP, 0:1], SM["wp2"][:, :], SM["g"][:, :], start=True, stop=True),
                r=["wp2", "g"], w=["PE2"])
            p.add("scalar", lambda e: e.activation(
                SM["logc"][:, :], PE2[0:NEXP, 0:1], AF.Identity, bias=SM["bp2"][:, 0:1]),
                r=["PE2", "bp2"], w=["logc"], hard=True)

          def s3():
            p.add("tensor", lambda e: e.transpose(
                PD[0:1, 0:NEXP], SM["logc"][:, :], SM["identf"][0:NEXP, 0:NEXP]),
                r=["logc", "identf"], w=["PD"])
            p.add("scalar", lambda e: e.copy(SM["row"][:, :], PD[0:1, 0:NEXP]),
                  r=["PD"], w=["row"], hard=True)
            p.add("vector", lambda e: e.tensor_reduce(
                SM["m"][:, :], SM["row"][:, :], axis=AX.X, op=ALU.max),
                r=["row"], w=["mr"], hard=True)
            p.add("vector", lambda e: e.tensor_scalar(
                SM["row2"][:, :], SM["row"][:, :], SM["m"][0:1, 0:1], -1.0,
                ALU.subtract, ALU.mult), r=["row", "mr"], w=["row2"], hard=True)
            p.add("vector", lambda e: e.scalar_tensor_tensor(
                SM["row3"][:, :], SM["iota8"][:, :], 1e-20, SM["row2"][:, :],
                ALU.mult, ALU.add),
                r=["row2", "iota8"], w=["row3"], hard=True)
            p.add("vector", lambda e: e.tensor_reduce(
                SM["umin"][:, :], SM["row3"][:, :], axis=AX.X, op=ALU.min),
                r=["row3"], w=["umin"], hard=True)
            p.add("vector", lambda e: e.tensor_single_scalar(
                SM["row2"][:, :], SM["row3"][:, :], SM["umin"][0:1, 0:1], ALU.is_equal),
                r=["row3", "umin"], w=["row2"], hard=True)
            p.add("vector", lambda e: e.tensor_mul(
                SM["row3"][:, :], SM["row2"][:, :], SM["iota8"][:, :]),
                r=["row2", "iota8"], w=["row3"], hard=True)
            p.add("vector", lambda e: e.reduce_sum(
                SM["self"][:, :], SM["row3"][:, :], axis=AX.X), r=["row3"], w=["self"], hard=True)
            p.add("vector", lambda e: e.tensor_copy(SM["seli"][:, :], SM["self"][:, :]),
                  r=["self"], w=["seli"], hard=True)
            if b == 0 and "d_row" in dbg_out:
                p.add("sync", lambda e: e.dma_start(dbg_out["d_row"][:, :], SM["row"][:, :]),
                      r=["row"], dma=True)
            if b == 0 and "d_sel" in dbg_out:
                p.add("sync", lambda e: e.dma_start(dbg_out["d_sel"][:, :], SM["seli"][:, :]),
                      r=["seli"], dma=True)

          def s4():
            def g_w1(e):
                e.reg_load(greg1, SM["seli"][0:1, 0:1])
                e.reg_mul(greg1, greg1, D * FF)
                return e.dma_start(
                    R1[:, :].rearrange("q (ct f) -> q ct f", ct=NCT),
                    bass.AP(W["W1"], greg1, [[FF, 128], [128 * FF, NCT], [1, FF]]))
            p.add("gpsimd", g_w1, r=["seli"], w=["R1", "R1h0", "R1h1"], dma=True)

            def g_b1(e):
                e.reg_load(greg1, SM["seli"][0:1, 0:1])
                e.reg_mul(greg1, greg1, FF)
                return e.dma_start(SMb1g[:, :], bass.AP(W["b1"], greg1, [[128, 16], [1, 128]]))
            p.add("gpsimd", g_b1, r=["seli"], w=["b1g", "X1L"], dma=True)

            def g_b2(e):
                e.reg_load(greg1, SM["seli"][0:1, 0:1])
                e.reg_mul(greg1, greg1, D)
                return e.dma_start(SMb2g[:, :], bass.AP(W["b2"], greg1, [[128, NEXP], [1, 128]]))
            p.add("gpsimd", g_b2, r=["seli"], w=["b2g", "X1L"], dma=True)

          def s5():
            p.add("tensor", lambda e: e.transpose(
                PE2[:, 0:16], SMb1g[:, :], SM["identf"][0:16, 0:16]),
                r=["b1g", "identf"], w=["PE2"])
            p.add("scalar", lambda e: e.copy(SM["b1c"][:, :], PE2[:, 0:16]),
                  r=["PE2"], w=["b1c"], hard=True)
            p.add("tensor", lambda e: e.transpose(
                PE2[:, 0:NEXP], SMb2g[:, :], SM["identf"][0:NEXP, 0:NEXP]),
                r=["b2g", "identf"], w=["PE2"])
            p.add("scalar", lambda e: e.copy(SM["b2c"][:, :], PE2[:, 0:NEXP]),
                  r=["PE2"], w=["b2c"], hard=True)
          return {0: s1, 1: s2, 2: s3, 3: s4, 5: s5}

        XIN0 = NCT * D

        def emit_ffn_load(k):
            xo = XIN0 + (k % 2) * (NCT * CHUNK)
            xka, xkb = "R4x%da" % (k % 2), "R4x%db" % (k % 2)
            toksR4 = p.cur("R4")
            p.add("sync", lambda e, k=k, xo=xo: e.dma_start(
                R4[:, xo:xo + 4 * CHUNK].rearrange("q (ct z) -> q ct z", ct=4),
                xlt_scr[0:4, :, k * CHUNK:(k + 1) * CHUNK].rearrange("ct q z -> q ct z")),
                r=["xlt" + str(k)], w=[xka, "R4"], dma=True)
            p.add("scalar", lambda e, k=k, xo=xo: e.dma_start(
                R4[:, xo + 4 * CHUNK: xo + 8 * CHUNK].rearrange("q (ct z) -> q ct z", ct=4),
                xlt_scr[4:8, :, k * CHUNK:(k + 1) * CHUNK].rearrange("ct q z -> q ct z")),
                r=["xlt" + str(k)], w=[xkb], deps=toksR4, dma=True)

        def emit_ffn(k, pre=None):
            xo = XIN0 + (k % 2) * (NCT * CHUNK)
            xk = ["R4x%da" % (k % 2), "R4x%db" % (k % 2)]
            for ft in range(16):
                hh = (ft % 2) * 512
                hk = "PA" + str(ft % 2)

                def hf(e, ft=ft, hh=hh, xo=xo):
                    for ct in range(NCT):
                        m = e.matmul(PA[:, hh:hh + 512],
                                     R1[:, ct * FF + ft * 128: ct * FF + (ft + 1) * 128],
                                     R4[:, xo + ct * CHUNK: xo + (ct + 1) * CHUNK],
                                     start=(ct == 0), stop=(ct == NCT - 1))
                    return m
                p.add("tensor", hf, r=["R1"] + xk, w=[hk])
                p.add("scalar", lambda e, ft=ft, hh=hh: e.activation(
                    R3[:, ft * CHUNK:(ft + 1) * CHUNK], PA[:, hh:hh + 512], AF.Gelu,
                    bias=SM["b1c"][:, ft:ft + 1]), r=[hk, "b1c"], w=["R3h" + str(ft % 4)])
            if pre is not None:
                emit_ffn_load(pre)
            if b == 0 and k == 0 and "d_h0" in dbg_out:
                p.add("sync", lambda e: e.dma_start(dbg_out["d_h0"][:, :], R3[:, 0:NH * CHUNK]),
                      r=["R3h0", "R3h1", "R3h2", "R3h3"], dma=True)
            for ht in range(NCT):
                rr = (ht % 2) * 512
                rk = "PB" + str(ht % 2)

                def rfx(e, ht=ht, rr=rr):
                    for ft in range(16):
                        m = e.matmul(PB[:, rr:rr + 512],
                                     R2[:, ft * D + ht * 128: ft * D + (ht + 1) * 128],
                                     R3[:, ft * CHUNK:(ft + 1) * CHUNK],
                                     start=(ft == 0), stop=(ft == 15))
                    return m
                p.add("tensor", rfx,
                      r=["R2p%d" % i for i in range(NPAIR)] + ["R3h0", "R3h1", "R3h2", "R3h3"],
                      w=[rk])
                p.add("vector", lambda e, ht=ht, rr=rr: e.tensor_scalar_add(
                    R3[:, NH * CHUNK + ht * CHUNK: NH * CHUNK + (ht + 1) * CHUNK],
                    PB[:, rr:rr + 512], SM["b2c"][:, ht:ht + 1]),
                    r=[rk, "b2c"], w=["R3r"])
            for et in range(NCT):
                oo = (et % 2) * 512
                ok = "PC" + str(et % 2)

                def of(e, et=et, oo=oo):
                    for ht in range(NCT):
                        m = e.matmul(PC[:, oo:oo + 512],
                                     R4[:, ht * D + et * 128: ht * D + (et + 1) * 128],
                                     R3[:, NH * CHUNK + ht * CHUNK:
                                        NH * CHUNK + (ht + 1) * CHUNK],
                                     start=(ht == 0), stop=(ht == NCT - 1))
                    return m
                p.add("tensor", of, r=["R4", "R4w1", "R4w2", "R3r"], w=[ok])
                p.add("vector", lambda e, et=et, oo=oo: e.tensor_scalar_add(
                    R9[:, et * 512:(et + 1) * 512], PC[:, oo:oo + 512],
                    SM["boFc"][:, et:et + 1]),
                    r=[ok, "boFc"], w=["R9q" + str(et // 2)])
            for u in range(4):
                st = 4 * k + u
                pbj, pjk = (PDb, "PD") if u % 2 == 0 else (PE2b, "PE2")

                def trf2(e, u=u, pbj=pbj):
                    for et in range(NCT):
                        m = e.transpose(pbj[:, et * 128:(et + 1) * 128],
                                        R9[:, et * 512 + u * 128: et * 512 + (u + 1) * 128],
                                        SM["identb"][:, :])
                    return m
                p.add("tensor", trf2, r=["R9q0", "R9q1", "R9q2", "R9q3", "identb"], w=[pjk])
                for qi in range(4):
                    eng = ("sync", "scalar", "sync", "scalar")[qi]
                    res = ("R8aL", "R8aL2", "R8aH", "R8aH2")[qi]
                    p.add(eng, lambda e, k=k, u=u, qi=qi: e.dma_start(
                        R8a[qi * 32:(qi + 1) * 32, :],
                        x1_scr[k * CHUNK + u * 128 + qi * 32:
                               k * CHUNK + u * 128 + (qi + 1) * 32, :]),
                        r=["x1s" + str(st)], w=[res], dma=True)
                p.add("scalar", lambda e, st=st: e.activation(
                    R8b[:, :], R8a[:, :], AF.Identity, bias=SM["m2r2"][:, st:st + 1],
                    scale=SM["r2p1"][:, st:st + 1]),
                    r=["R8aL", "R8aL2", "R8aH", "R8aH2"] + ["m2r2", "r2p1"], w=["R8bL", "R8bL2", "R8bH", "R8bH2"])
                p.add("vector", lambda e, pbj=pbj: e.scalar_tensor_tensor(
                    R8b[:, :], pbj[:, 0:1024], 0.3, R8b[:, :], ALU.mult, ALU.add),
                    r=[pjk] + ["R8bL", "R8bL2", "R8bH", "R8bH2"], w=["R8bL", "R8bL2", "R8bH", "R8bH2"])
                p.add("scalar", lambda e, b=b, k=k, u=u: e.dma_start(
                    out_d[b, k * CHUNK + u * 128: k * CHUNK + (u + 1) * 128, :], R8b[:, :]),
                    r=["R8bL", "R8bL2", "R8bH", "R8bH2"], dma=True)

        emit_attn(3)
        beta3 = emit_tailA_beta(3)
        emit_attn(0, mid_hooks={
            t: (lambda t=t: emit_tailA_bcf(3, beta3[0], beta3[1], beta3[2], t))
            for t in range(NPAIR)})
        if b == 0 and "d_ctxf3" in dbg_out:
            p.add("sync", lambda e: e.dma_start(dbg_out["d_ctxf3"][:, :], R7f[:, :]),
                  r=["R7f"], dma=True)
        emit_tailB(3)
        beta0 = emit_tailA_beta(0)
        emit_attn(1, hooks=router_stages(), mid_hooks={
            t: (lambda t=t: emit_tailA_bcf(0, beta0[0], beta0[1], beta0[2], t))
            for t in range(NPAIR)})
        emit_tailB(0)
        beta1 = emit_tailA_beta(1)
        def g_w2_half(h):
            def g(e, h=h):
                e.reg_load(greg1, SM["seli"][0:1, 0:1])
                e.reg_mul(greg1, greg1, FF * D)
                e.reg_add(greg1, greg1, h * 8 * 128 * D)
                return e.dma_start(
                    R2[:, h * 8192:(h + 1) * 8192].rearrange("q (ft d) -> q ft d", ft=8),
                    bass.AP(W["W2"], greg1, [[D, 128], [128 * D, 8], [1, D]]))
            p.add("gpsimd", g, r=["seli"],
                  w=["R2p%d" % t for t in range(h * 4, h * 4 + 4)], dma=True)

        def attn2_hook(t):
            emit_tailA_bcf(1, beta1[0], beta1[1], beta1[2], t)
            if t == 3:
                g_w2_half(0)
            elif t == 7:
                g_w2_half(1)
        emit_attn(2, mid_hooks={t: (lambda t=t: attn2_hook(t)) for t in range(NPAIR)})
        emit_tailB(1)
        emit_tailA(2)
        toks_pv = p.cur("R4")
        p.add("sync", lambda e: e.dma_start(R4[:, 0:4096], wtF_scr[:, 0:4096]),
              r=["wtF"], w=["R4w1"], deps=toks_pv, dma=True)
        p.add("scalar", lambda e: e.dma_start(R4[:, 4096:NCT * D], wtF_scr[:, 4096:NCT * D]),
              r=["wtF"], w=["R4w2"], deps=toks_pv, dma=True)
        if b == 0 and "d_w1s" in dbg_out:
            p.add("sync", lambda e: e.dma_start(dbg_out["d_w1s"][:, :], R1[:, :]),
                  r=["R1"], dma=True)
        if b == 0 and "d_x1" in dbg_out:
            p.add("sync", lambda e: e.dma_start(dbg_out["d_x1"][:, :], x1_scr[:, :]),
                  r=[f"x1s{st}" for st in range(16)], dma=True)
        if b == 0 and "d_xlt" in dbg_out:
            p.add("sync", lambda e: e.dma_start(dbg_out["d_xlt"][:, :, :], xlt_scr[:, :, :]),
                  r=["xlt0", "xlt1", "xlt2", "xlt3"], dma=True)

        emit_ffn_load(3)
        emit_ffn(3, pre=0)
        emit_tailB(2)
        emit_ffn(0, pre=1)
        emit_ffn(1, pre=2)
        emit_ffn(2)

    p.emit(nc, semh)
    stack.close()
    return nc, sorted(dbg_out.keys())


_CACHE = {}


def _get_prog(nb, dbg):
    key = (nb, tuple(sorted(dbg)))
    if key not in _CACHE:
        _CACHE[key] = build_program(nb, dbg)
    return _CACHE[key]


def _in_maps(inputs, nb, ncores):
    cst = _host_consts()
    x = np.ascontiguousarray(np.asarray(inputs["L0_attn_x"], dtype=np.float32))
    base = {
        "Wq": inputs["L0_attn_Wq"], "bq": inputs["L0_attn_bq"],
        "Wk": inputs["L0_attn_Wk"], "bk": inputs["L0_attn_bk"],
        "Wv": inputs["L0_attn_Wv"], "bv": inputs["L0_attn_bv"],
        "Wo": inputs["L0_attn_Wo"], "bo": inputs["L0_attn_bo"],
        "Wp1": inputs["L0_ffn_Wp1"], "bp1": inputs["L0_ffn_bp1"],
        "Wp2": inputs["L0_ffn_Wp2"], "bp2": inputs["L0_ffn_bp2"],
        "W1": inputs["L0_ffn_W1"], "b1": inputs["L0_ffn_b1"],
        "W2": inputs["L0_ffn_W2"], "b2": inputs["L0_ffn_b2"],
        "WoF": inputs["L0_ffn_Wo"], "boF": inputs["L0_ffn_bo"],
    }
    base = {k: np.ascontiguousarray(np.asarray(v, dtype=np.float32)) for k, v in base.items()}
    base.update(cst)
    maps = []
    for cc in range(ncores):
        m = dict(base)
        m["x"] = np.ascontiguousarray(x[cc * nb:(cc + 1) * nb])
        maps.append(m)
    return maps


def kernel(**inputs):
    nc, _ = _get_prog(NB, tuple(DBG))
    maps = _in_maps(inputs, NB, NCORES)
    res = run_bass_kernel_spmd(nc, maps, core_ids=list(range(NCORES)))
    out = np.concatenate([np.asarray(r["out"]) for r in res.results], axis=0)
    return np.ascontiguousarray(out.astype(np.float32))


def _run_debug(inputs, nb=1, dbg=(), ncores=1):
    nc, names = _get_prog(nb, tuple(dbg))
    maps = _in_maps(inputs, nb, ncores)
    res = run_bass_kernel_spmd(nc, maps, core_ids=list(range(ncores)))
    return res.results[0]


# version 21
# speedup vs baseline: 1.0429x; 1.0090x over previous
import numpy as np
import ml_dtypes

import concourse.bass as bass
import concourse.mybir as mybir
from concourse.bass_utils import run_bass_kernel_spmd

F32 = mybir.dt.float32
BF16 = mybir.dt.bfloat16
I32 = mybir.dt.int32
AF = mybir.ActivationFunctionType
ALU = mybir.AluOpType
AX = mybir.AxisListType

NCORES = 8
NB = 4
S = 2048
D = 1024
NH = 16
NPAIR = 8
NST = 16
NCT = 8
FF = 2048
NEXP = 8
NAV = 128
EPS_LN = 1e-5
EPS_SM = 1e-8
CHUNK = 512
CHUNK_ORDER = (3, 0, 1, 2)

DBG = []
FUSE_PV = True


def _host_consts():
    i = np.arange(32, dtype=np.float64)
    inv = 10000.0 ** (-2.0 * i / 64.0)
    pos = np.arange(S, dtype=np.float64)
    ang = pos[None, :] * inv[:, None]
    cos = np.cos(ang)
    sin = np.sin(ang)
    cosT = np.empty((128, S), np.float32)
    sinT = np.empty((128, S), np.float32)
    for p_ in range(128):
        k = (p_ % 64) // 2
        cosT[p_] = cos[k]
        sinT[p_] = sin[k] if (p_ % 2) else -sin[k]
    perm = np.zeros((128, 128), np.float32)
    for r in range(0, 128, 2):
        perm[r + 1, r] = 1.0
        perm[r, r + 1] = 1.0
    tri = (np.arange(128)[:, None] <= np.arange(128)[None, :]).astype(np.float32)
    tri2 = np.concatenate([tri, tri], axis=1).astype(ml_dtypes.bfloat16)
    bsel = np.zeros((48, 8 * 128), np.float32)
    for base in (0, 32):
        for t in range(8):
            bsel[base + 2 * t, t * 128: t * 128 + 64] = 1.0
            bsel[base + 2 * t + 1, t * 128 + 64: (t + 1) * 128] = 1.0
    return dict(cosT=cosT, sinT=sinT,
                ident_bf=np.eye(128).astype(ml_dtypes.bfloat16),
                ident_f=np.eye(128).astype(np.float32),
                perm_bf=perm.astype(ml_dtypes.bfloat16), tri2=tri2,
                iota8=np.arange(8, dtype=np.float32)[None, :],
                bsel=bsel.astype(ml_dtypes.bfloat16))


STREAMS = ("tensor", "scalar", "vector", "gpsimd", "sync")
CSEM = {"tensor": "pe", "scalar": "act", "vector": "dve", "gpsimd": "pool", "sync": "syn"}
DSEM = {"scalar": "dsc", "gpsimd": "dgp", "sync": "dsy"}
NDG = {"sync": 8, "scalar": 8, "gpsimd": 4}
SEMS = ("pe", "act", "dve", "pool") + tuple(
    f"{DSEM[s]}{i}" for s in NDG for i in range(NDG[s]))


class Prog:

    def __init__(self):
        self.q = {s: [] for s in STREAMS}
        self.cnt = {}
        self.dma_n = {}
        self.dma_tok = {}
        self.last_w = {}
        self.readers = {}

    def add(self, stream, fn, r=(), w=(), deps=(), dma=False, hard=False):
        dep = set(d for d in deps if d is not None)
        if dma:
            base = DSEM[stream]
            ndg = NDG[stream]
            n = self.dma_n.get(stream, 0)
            self.dma_n[stream] = n + 1
            sem = f"{base}{n % ndg}"
            inc = 16
            if n >= ndg:
                dep.add(self.dma_tok[(stream, (n - ndg) % ndg)])
        else:
            sem = CSEM[stream]
            inc = 1
        cur = self.cnt.get(sem, 0) + inc
        self.cnt[sem] = cur
        tok = (sem, cur)
        if dma:
            self.dma_tok[(stream, (self.dma_n[stream] - 1) % NDG[stream])] = tok
        for res in r:
            if res in self.last_w:
                dep.add(self.last_w[res])
        for res in w:
            if res in self.last_w:
                dep.add(self.last_w[res])
            for t in self.readers.get(res, ()):
                dep.add(t)
        for res in r:
            self.readers.setdefault(res, []).append(tok)
        for res in w:
            self.last_w[res] = tok
            self.readers[res] = []
        dep.discard(tok)
        self.q[stream].append((sorted(dep), fn, sem, inc, hard))
        return tok

    def cur(self, res):
        toks = list(self.readers.get(res, ()))
        if res in self.last_w:
            toks.append(self.last_w[res])
        return toks

    def emit(self, nc, semh):
        totals = dict(self.cnt)
        with nc.Block() as block:
            for stream in STREAMS:
                tasks = self.q[stream]
                own = CSEM[stream]

                def body(eng, tasks=tasks, own=own, stream=stream):
                    guar = {}
                    nown = 0
                    for deps, fn, sem, inc, hard in tasks:
                        for (s2, v2) in deps:
                            if guar.get(s2, 0) < v2:
                                eng.wait_ge(semh[s2], v2)
                                guar[s2] = v2
                        fn(eng).then_inc(semh[sem], inc)
                        if sem == own:
                            nown += 1
                            guar[own] = max(guar.get(own, 0), 0)
                    if stream == "sync":
                        for s2 in SEMS:
                            if s2 in totals and guar.get(s2, 0) < totals[s2]:
                                eng.wait_ge(semh[s2], totals[s2])

                getattr(block, stream)(body)


def build_program(nb=NB, dbg=()):
    nc = bass.Bass("TRN2", target_bir_lowering=False, debug=False, num_swdge_queues=4)

    x_in = nc.dram_tensor("x", [nb, S, D], F32, kind="ExternalInput")
    W = {}
    for n, shp in [("Wq", [D, D]), ("bq", [D]), ("Wk", [D, D]), ("bk", [D]),
                   ("Wv", [D, D]), ("bv", [D]), ("Wo", [D, D]), ("bo", [D]),
                   ("Wp1", [D, NAV]), ("bp1", [NAV]), ("Wp2", [NAV, NEXP]), ("bp2", [NEXP]),
                   ("W1", [NEXP, D, FF]), ("b1", [NEXP, FF]), ("W2", [NEXP, FF, D]),
                   ("b2", [NEXP, D]), ("WoF", [D, D]), ("boF", [D])]:
        W[n] = nc.dram_tensor(n, shp, F32, kind="ExternalInput")
    C = {}
    for n, shp, dt in [("cosT", [128, S], F32), ("sinT", [128, S], F32),
                       ("ident_bf", [128, 128], BF16), ("ident_f", [128, 128], F32),
                       ("perm_bf", [128, 128], BF16), ("tri2", [128, 256], BF16),
                       ("bsel", [48, 8 * 128], BF16),
                       ("iota8", [1, 8], F32)]:
        C[n] = nc.dram_tensor(n, shp, dt, kind="ExternalInput")
    out_d = nc.dram_tensor("out", [nb, S, D], F32, kind="ExternalOutput")

    x1_scr = nc.dram_tensor("x1_scr", [S, D], F32, kind="Internal")
    xlt_scr = nc.dram_tensor("xlt_scr", [NCT, 128, S], BF16, kind="Internal")
    wt_scr = nc.dram_tensor("wt_scr", [4, 128, NCT * D], BF16, kind="Internal")
    wtF_scr = nc.dram_tensor("wtF_scr", [128, NCT * D], BF16, kind="Internal")

    dbg_out = {}
    for name, shp, dt in [
        ("d_xlnT", [128, NCT * S], BF16), ("d_QT", [128, NCT * S], BF16),
        ("d_KT", [128, NCT * S], BF16), ("d_Vt", [128, NST * 1040], BF16),
        ("d_ctx3", [128, NPAIR * CHUNK], BF16), ("d_sums3", [16, CHUNK], F32),
        ("d_beta3", [16, CHUNK], F32), ("d_ctxf3", [128, NPAIR * CHUNK], BF16),
        ("d_x1", [S, D], F32), ("d_row", [1, 8], F32), ("d_sel", [1, 1], I32),
        ("d_w1s", [128, NCT * FF], BF16), ("d_h0", [128, NH * CHUNK], BF16),
        ("d_xlt", [NCT, 128, S], BF16), ("d_lastln", [128, NCT], F32),
        ("d_b1c", [128, 16], F32), ("d_b2c", [128, NEXP], F32),
        ("d_xin0", [128, NCT * CHUNK], BF16), ("d_rs0", [128, NCT * CHUNK], BF16),
        ("d_hraw", [128, CHUNK], F32), ("d_b1g", [16, 128], F32),
        ("d_b1c_early", [128, 16], F32), ("d_r2", [128, NST], F32),
        ("d_m2r2", [128, NST], F32), ("d_cb", [128, 16], F32),
        ("d_pdsnap", [128, 16], F32), ("d_b1gsnap", [16, 128], F32),
        ("d_idf", [128, 128], F32), ("d_idb", [128, 128], BF16),
    ]:
        if name in dbg:
            dbg_out[name] = nc.dram_tensor(name, shp, dt, kind="ExternalOutput")

    off = [16384 + 128]

    def place(name, shape, dtype, at=None):
        esz = 2 if dtype == BF16 else 4
        nbytes = int(np.prod(shape[1:])) * esz
        if at is None:
            at = off[0]
            off[0] = at + ((nbytes + 31) // 32) * 32
        return nc.alloc_sbuf_tensor_at(name, shape, dtype, offset=at)

    R1 = place("R1", [128, NCT * S], BF16)
    R2 = place("R2", [128, NCT * S], BF16)
    R3 = place("R3", [128, NCT * S], BF16)
    R4 = place("R4", [128, NST * 1040], BF16)
    R5 = place("R5", [128, NCT * D], BF16)
    r6 = off[0]
    R6c = place("R6c", [128, NPAIR * CHUNK], BF16)
    R6p = place("R6p", [128, 2048], BF16)
    R6tab = place("R6tab", [128, S], F32, at=r6)
    r7 = off[0]
    R7f = place("R7f", [128, NPAIR * CHUNK], BF16)
    R7tab = place("R7tab", [128, S], F32, at=r7)
    off[0] = max(off[0], r7 + 4 * S)
    R8a = place("R8a", [128, D], F32)
    R8b = place("R8b", [128, D], F32)
    R9 = place("R9", [128, 2 * S], BF16)
    R10a = place("R10a", [128, D], F32)
    R10b = place("R10b", [128, D], F32)
    X1L = place("X1L", [128, NCT * CHUNK], BF16)
    x1l_at = X1L.manual_sbuf_range[0] if X1L.manual_sbuf_range else 0
    SMb1g = nc.alloc_sbuf_tensor_at("SM_b1g", [16, 128], F32, offset=x1l_at + 2048)
    SMb2g = nc.alloc_sbuf_tensor_at("SM_b2g", [NEXP, 128], F32, offset=x1l_at + 2560)
    SMbvrow = nc.alloc_sbuf_tensor_at("SM_bvrow", [1, D], BF16, offset=X1L.manual_sbuf_range[0]
                                      if X1L.manual_sbuf_range else 0)

    SM = {}
    for n, shp, dt in [
        ("r2", [128, NST], F32), ("m2r2", [128, NST], F32), ("r2p1", [128, NST], F32),
        ("stats", [128, 24], F32), ("mv", [128, 4], F32), ("ln1", [128, 6], F32),
        ("bqc", [128, NCT], F32), ("bkc", [128, NCT], F32), ("boc", [128, NCT], F32),
        ("boFc", [128, NCT], F32), ("b2c", [128, NCT], F32),
        ("cb0", [128, 8], F32), ("b1c", [128, 16], F32), ("cb1", [128, 8], F32),
        ("bp1", [128, 1], F32), ("wp2", [128, NEXP], F32), ("bp2", [NEXP, 1], F32),
        ("tri", [128, 256], BF16), ("identb", [128, 128], BF16),
        ("identf", [128, 128], F32), ("perm", [128, 128], BF16), ("iota8", [1, 8], F32),
        ("ones1b", [1, 128], BF16), ("ones1f", [1, 128], F32),
        ("row", [1, 8], F32), ("row2", [1, 8], F32), ("row3", [1, 8], F32),
        ("m", [1, 1], F32), ("umin", [1, 1], F32), ("self", [1, 1], F32), ("seli", [1, 1], I32),
        ("g", [128, 1], F32), ("lastT", [128, NCT], F32), ("lastln", [128, NCT], F32),
        ("epsln", [128, 1], F32), ("epssm", [128, 1], F32),
        ("bsel", [48, 8 * 128], BF16), ("bhi", [16, CHUNK], BF16),
        ("sc1", [1, 2], F32), ("bc1", [128, 2], F32),
        ("logc", [NEXP, 1], F32),
        ("xln2", [128, D], BF16), ("xln2b", [128, D], BF16),
    ]:
        SM[n] = place("SM_" + n, shp, dt)
    assert off[0] <= 16384 + 212576 - 64, f"sbuf overflow: {off[0]}"

    PA = nc.alloc_psum_tensor("PA", [128, 1024], F32)
    PB = nc.alloc_psum_tensor("PB", [128, 1024], F32)
    PC = nc.alloc_psum_tensor("PC", [128, 1024], F32)
    PD = nc.alloc_psum_tensor("PD", [128, 512], F32)
    PE2 = nc.alloc_psum_tensor("PE2", [128, 512], F32)
    PDb = PD.bitcast(BF16)
    PE2b = PE2.bitcast(BF16)

    p = Prog()
    from contextlib import ExitStack
    stack = ExitStack()
    semh = {s: stack.enter_context(nc.semaphore(s)) for s in SEMS}
    greg1 = stack.enter_context(nc.gpsimd.register("greg1"))

    p.add("sync", lambda e: e.dma_start(SM["tri"][:, :], C["tri2"][:, :]), w=["tri"], dma=True)
    p.add("sync", lambda e: e.dma_start(SM["identb"][:, :], C["ident_bf"][:, :]), w=["identb"], dma=True)
    p.add("sync", lambda e: e.dma_start(SM["identf"][:, :], C["ident_f"][:, :]), w=["identf"], dma=True)
    p.add("sync", lambda e: e.dma_start(SM["perm"][:, :], C["perm_bf"][:, :]), w=["perm"], dma=True)
    p.add("sync", lambda e: e.dma_start(SM["iota8"][:, :], C["iota8"][:, :]), w=["iota8"], dma=True)
    p.add("sync", lambda e: e.dma_start(SM["bsel"][:, :], C["bsel"][:, :]), w=["bsel"], dma=True)
    p.add("sync", lambda e: e.dma_start(
        SM["bp1"][:, :], W["bp1"].ap().rearrange("(p o) -> p o", o=1)), w=["bp1"], dma=True)
    p.add("sync", lambda e: e.dma_start(SM["wp2"][:, :], W["Wp2"][:, :]), w=["wp2"], dma=True)
    p.add("sync", lambda e: e.dma_start(
        SM["bp2"][:, :], W["bp2"].ap().rearrange("(p o) -> p o", o=1)), w=["bp2"], dma=True)
    p.add("vector", lambda e: e.memset(SM["ones1b"][:, :], 1.0), w=["ones1b"])
    p.add("vector", lambda e: e.memset(SM["epsln"][:, :], EPS_LN), w=["epsln"])
    p.add("vector", lambda e: e.memset(SM["epssm"][:, :], EPS_SM), w=["epssm"])
    p.add("vector", lambda e: e.memset(SM["ones1f"][:, :], 1.0), w=["ones1f"])

    for i, wkey in enumerate(("Wq", "Wk", "Wv", "Wo")):
        p.add("gpsimd", lambda e, wkey=wkey: e.dma_start(
            R5[:, :].rearrange("q (dt c) -> q dt c", dt=NCT),
            W[wkey].ap().rearrange("(dt q) c -> q dt c", q=128)),
            w=["R5"], dma=True)
        for ct in range(NCT):
            pb = PDb if ct % 2 == 0 else PE2b
            pk = "PD" if ct % 2 == 0 else "PE2"

            def tf(e, ct=ct, pb=pb):
                for dt in range(NCT):
                    m = e.transpose(pb[:, dt * 128:(dt + 1) * 128],
                                    R5[:, dt * D + ct * 128: dt * D + (ct + 1) * 128],
                                    SM["identb"][:, :])
                return m
            p.add("tensor", tf, r=["R5", "identb"], w=[pk])
            p.add("scalar", lambda e, ct=ct, pb=pb: e.copy(
                R3[:, ct * D:(ct + 1) * D], pb[:, 0:1024]), r=[pk], w=["R3"])
        p.add("scalar", lambda e, i=i: e.dma_start(wt_scr[i, :, :], R3[:, 0:NCT * D]),
              r=["R3"], w=[f"wt{i}"], dma=True)

    R5f32 = R5.bitcast(F32)
    for hh in range(2):
        p.add("sync", lambda e, hh=hh: e.dma_start(
            R5f32[:, 0:4096].rearrange("q (ht ee) -> q ht ee", ht=4),
            W["WoF"].ap().rearrange("(ht q) ee -> q ht ee", q=128)[:, hh * 4:hh * 4 + 4, :]),
            w=["R5"], dma=True)
        p.add("vector", lambda e: e.tensor_copy(R3[:, 0:4096], R5f32[:, 0:4096]),
              r=["R5"], w=["R3"])
        p.add("scalar", lambda e, hh=hh: e.dma_start(
            wtF_scr[:, hh * 4096:(hh + 1) * 4096], R3[:, 0:4096]),
            r=["R3"], w=["wtF"], dma=True)

    for bkey, dstk in (("bq", "bqc"), ("bk", "bkc"), ("bo", "boc"), ("boF", "boFc")):
        p.add("sync", lambda e, bkey=bkey: e.dma_start(
            R10a[0:1, 0:1024], W[bkey].ap().rearrange("(o d) -> o d", o=1)),
            w=["R10a"], dma=True)

        def tf(e):
            for ct in range(NCT):
                m = e.transpose(PD[:, ct:ct + 1], R10a[0:1, ct * 128:(ct + 1) * 128],
                                SM["identf"][0:1, 0:1])
            return m
        p.add("tensor", tf, r=["R10a", "identf"], w=["PD"])
        p.add("scalar", lambda e, dstk=dstk: e.copy(SM[dstk][:, 0:NCT], PD[:, 0:NCT]),
              r=["PD"], w=[dstk])

    for b in range(nb):
        pend_copy = [None]
        for st in range(NST):
            buf, bk_ = (R8a, "R8a") if st % 2 == 0 else (R8b, "R8b")
            bkq = [bk_ + ("L", "L2", "H", "H2")[i] for i in range(4)]
            for qi in range(4):
                eng = "sync"
                p.add(eng, lambda e, b=b, st=st, buf=buf, qi=qi: e.dma_start(
                    buf[qi * 32:(qi + 1) * 32, :],
                    x_in[b, st * 128 + qi * 32: st * 128 + (qi + 1) * 32, :]),
                    w=[bkq[qi]], dma=True)

            q_ = st % 2
            XL = SM["xln2"] if q_ == 0 else SM["xln2b"]
            sq, mq, lq = 12 * q_, 2 * q_, 3 * q_
            rs = lambda n: n + str(q_)

            def stf(e, buf=buf, sq=sq):
                e.bn_stats(SM["stats"][:, sq:sq + 6], buf[:, 0:512])
                return e.bn_stats(SM["stats"][:, sq + 6:sq + 12], buf[:, 512:1024])
            p.add("vector", stf, r=bkq, w=[rs("stats")])
            p.add("vector", lambda e, sq=sq, mq=mq: e.bn_aggr(
                SM["mv"][:, mq:mq + 2], SM["stats"][:, sq:sq + 12]),
                r=[rs("stats")], w=[rs("mv")], hard=True)
            p.add("scalar", lambda e, mq=mq, lq=lq: e.activation(
                SM["ln1"][:, lq + 2:lq + 3], SM["mv"][:, mq + 1:mq + 2], AF.Ln,
                bias=SM["epsln"][:, 0:1]),
                r=[rs("mv"), "epsln"], w=[rs("ln1x")], hard=True)
            p.add("scalar", lambda e, lq=lq: e.activation(
                SM["ln1"][:, lq:lq + 1], SM["ln1"][:, lq + 2:lq + 3], AF.Exp, scale=-0.5),
                r=[rs("ln1x")], w=[rs("ln1")], hard=True)
            p.add("vector", lambda e, mq=mq, lq=lq: e.scalar_tensor_tensor(
                SM["ln1"][:, lq + 1:lq + 2], SM["mv"][:, mq:mq + 1], -1.0,
                SM["ln1"][:, lq:lq + 1],
                ALU.mult, ALU.mult), r=[rs("mv"), rs("ln1")], w=[rs("ln1b")], hard=True)
            if pend_copy[0] is not None:
                pend_copy[0]()
                pend_copy[0] = None
            p.add("scalar", lambda e, buf=buf, lq=lq, XL=XL: e.activation(
                XL[:, :], buf[:, :], AF.Identity,
                bias=SM["ln1"][:, lq + 1:lq + 2], scale=SM["ln1"][:, lq:lq + 1]),
                r=bkq + [rs("ln1"), rs("ln1b")], w=[rs("xln2")], hard=True)
            pb, pk = (PDb, "PD") if st % 2 == 0 else (PE2b, "PE2")

            def tf(e, pb=pb, XL=XL):
                for ct in range(NCT):
                    m = e.transpose(pb[:, ct * 128:(ct + 1) * 128],
                                    XL[:, ct * 128:(ct + 1) * 128], SM["identb"][:, :])
                return m
            p.add("tensor", tf, r=[rs("xln2"), "identb"], w=[pk])
            def emit_copy(st=st, pb=pb, pk=pk):
                p.add("vector", lambda e, st=st, pb=pb: e.tensor_copy(
                    R1[:, :].rearrange(
                        "q (ct s) -> q ct s", ct=NCT)[:, :, st * 128:(st + 1) * 128],
                    pb[:, 0:1024].rearrange("q (ct s) -> q ct s", ct=NCT)),
                    r=[pk], w=["R1h" + str(st // 8)])
            pend_copy[0] = emit_copy
        pend_copy[0]()
        pend_copy[0] = None

        if b == 0 and "d_xlnT" in dbg_out:
            p.add("sync", lambda e: e.dma_start(dbg_out["d_xlnT"][:, :], R1[:, :]),
                  r=["R1h0", "R1h1"], dma=True)

        p.add("sync", lambda e: e.dma_start(R6tab[:, :], C["cosT"][:, :]),
              w=["R6tab", "R6c"], dma=True)
        p.add("sync", lambda e: e.dma_start(R7tab[:, :], C["sinT"][:, :]),
              w=["R7tab", "R7f"], dma=True)

        def proj_rope(Wi, dstT, dstks, bck):
            p.add("sync", lambda e, Wi=Wi: e.dma_start(R5[:, :], wt_scr[Wi, :, :]),
                  r=[f"wt{Wi}"], w=["R5"], dma=True)

            def emit_perm_rope(unit):
                dt, qq = unit // 2, unit % 2
                o = 1024 * (unit % 2)
                rk = "R9q" + str(unit % 2)
                if unit % 2 == 0:
                    pd_ap, pe_ap, pdk, pek = PD[:, 0:512], PE2[:, 0:512], "PD", "PE2"
                else:
                    pd_ap, pe_ap, pdk, pek = PC[:, 0:512], PC[:, 512:1024], "PC0", "PC1"

                def permf(e, o=o, pd_ap=pd_ap, pe_ap=pe_ap):
                    e.matmul(pd_ap, SM["perm"][:, :], R9[:, o:o + 512],
                             start=True, stop=True)
                    return e.matmul(pe_ap, SM["perm"][:, :], R9[:, o + 512:o + 1024],
                                    start=True, stop=True)
                p.add("tensor", permf, r=[rk, "perm"], w=[pdk, pek])
                scol = qq * 1024

                def ropef1(e, o=o, scol=scol, pd_ap=pd_ap, pe_ap=pe_ap):
                    e.tensor_mul(R10a[:, 0:1024], R9[:, o:o + 1024],
                                 R6tab[:, scol:scol + 1024])
                    e.tensor_mul(R10b[:, 0:512], pd_ap, R7tab[:, scol:scol + 512])
                    return e.tensor_mul(R10b[:, 512:1024], pe_ap,
                                        R7tab[:, scol + 512:scol + 1024])
                p.add("vector", ropef1, r=[rk, pdk, pek, "R6tab", "R7tab"],
                      w=["R10a", "R10b", "R10bs0", "R10bs1", "R10bln0", "R10bln1",
                         "bhi", "blo"])
                p.add("vector", lambda e, dt=dt, scol=scol: e.tensor_add(
                    dstT[:, dt * S + scol: dt * S + scol + 1024],
                    R10a[:, 0:1024], R10b[:, 0:1024]),
                    r=["R10a", "R10b"], w=dstks(dt), hard=True)

            for dt in range(NCT):
                for qq in range(2):
                    unit = dt * 2 + qq
                    pb, pk = (PA, "PA") if unit % 2 == 0 else (PB, "PB")

                    def mf(e, dt=dt, qq=qq, pb=pb):
                        for ct in range(NCT):
                            for nn in range(2):
                                m = e.matmul(
                                    pb[:, nn * 512:(nn + 1) * 512],
                                    R5[:, ct * D + dt * 128: ct * D + (dt + 1) * 128],
                                    R1[:, ct * S + qq * 1024 + nn * 512:
                                       ct * S + qq * 1024 + (nn + 1) * 512],
                                    start=(ct == 0), stop=(ct == NCT - 1))
                        return m
                    p.add("tensor", mf, r=["R5", "R1h" + str(qq)], w=[pk + "0", pk + "1", pk])
                    o = 1024 * (unit % 2)
                    rk = "R9q" + str(unit % 2)
                    p.add("scalar", lambda e, pb=pb, dt=dt, o=o, bck=bck: e.activation(
                        R9[:, o:o + 1024], pb[:, 0:1024], AF.Identity,
                        bias=SM[bck][:, dt:dt + 1]),
                        r=[pk, pk + "0", pk + "1", bck], w=[rk])
                    if unit >= 1:
                        emit_perm_rope(unit - 1)
            emit_perm_rope(15)

        proj_rope(0, R2, lambda dt: ["R2p%d" % dt], "bqc")
        proj_rope(1, R3,
                  lambda dt: ["R3", "R3h0", "R3h1", "R3h2", "R3h3", "R3r"], "bkc")

        p.add("vector", lambda e: e.memset(
            R4[:, :].rearrange("q (t h c) -> q t h c", t=NST, h=NH)[:, :, :, 64:65], 1.0),
            w=["R4"])
        p.add("gpsimd", lambda e: e.dma_start(
            SMbvrow[:, :], W["bv"].ap().rearrange("(o d) -> o d", o=1)), w=["X1L"], dma=True)
        p.add("sync", lambda e: e.dma_start(R5[:, :], wt_scr[2, :, :]),
              r=["wt2"], w=["R5"], dma=True)
        for st in range(NST):
            pbv, pk = (PA, "PA") if st % 2 == 0 else (PB, "PB")

            def vf(e, st=st, pbv=pbv):
                for ct in range(NCT):
                    for nn in range(2):
                        e.matmul(pbv[:, nn * 512:(nn + 1) * 512],
                                 R1[:, ct * S + st * 128: ct * S + (st + 1) * 128],
                                 R5[:, ct * D + nn * 512: ct * D + (nn + 1) * 512],
                                 start=(ct == 0), stop=False)
                for nn in range(2):
                    m = e.matmul(pbv[:, nn * 512:(nn + 1) * 512], SM["ones1b"][:, :],
                                 SMbvrow[:, nn * 512:(nn + 1) * 512], start=False, stop=True)
                return m
            p.add("tensor", vf, r=["R5", "R1h" + str(st // 8), "ones1b", "X1L"],
                  w=[pk, pk + "0", pk + "1"])
            p.add("vector", lambda e, st=st, pbv=pbv: e.tensor_copy(
                R4[:, :].rearrange("q (t h c) -> q t h c", t=NST, h=NH)[:, st, :, 0:64],
                pbv[:, 0:1024].rearrange("q (h c) -> q h c", h=NH)),
                r=[pk, pk + "0", pk + "1"], w=["R4"])

        if b == 0 and "d_QT" in dbg_out:
            p.add("sync", lambda e: e.dma_start(dbg_out["d_QT"][:, :], R2[:, :]),
                  r=["R2p%d" % i for i in range(NPAIR)], dma=True)
        if b == 0 and "d_KT" in dbg_out:
            p.add("sync", lambda e: e.dma_start(dbg_out["d_KT"][:, :], R3[:, :]), r=["R3"], dma=True)
        if b == 0 and "d_Vt" in dbg_out:
            p.add("sync", lambda e: e.dma_start(dbg_out["d_Vt"][:, :], R4[:, :]), r=["R4"], dma=True)

        p.add("sync", lambda e: e.dma_start(R5[:, :], wt_scr[3, :, :]),
              r=["wt3"], w=["R5"], dma=True)

        R10bb = R10b.bitcast(BF16)
        tail_tt = {}
        tail_beta = [None]

        def emit_attn(c, hooks=None, mid_hooks=None):
            pi = 32 * (c % 2)
            for t in range(NPAIR):
                njs = 4 * c + 4

                def sc_mm(e, t, j, qoff, w_, stb):
                    kb = t * S + j * 128
                    qb = t * S + c * CHUNK + qoff
                    e.matmul(stb[:, qoff:CHUNK], R3[0:64, kb:kb + 128],
                             R2[0:64, qb:qb + w_], start=True, stop=True,
                             tile_position=(0, 0))
                    return e.matmul(stb[:, CHUNK + qoff:2 * CHUNK], R3[64:128, kb:kb + 128],
                                    R2[64:128, qb:qb + w_], start=True, stop=True,
                                    tile_position=(64, 0))

                def pv_mm(e, t, j, qoff, w_, po, first, lastj):
                    vb = j * 1040 + (2 * t) * 65
                    e.matmul(PC[0:65, qoff:CHUNK], R4[:, vb:vb + 65],
                             R6p[:, po + qoff:po + qoff + w_], start=first, stop=lastj)
                    return e.matmul(PC[0:65, CHUNK + qoff:2 * CHUNK],
                                    R4[:, vb + 65:vb + 130],
                                    R6p[:, po + CHUNK + qoff:po + CHUNK + qoff + w_],
                                    start=first, stop=lastj)

                def piece(j):
                    qoff = max(0, 128 * j - CHUNK * c)
                    return qoff, CHUNK - qoff, (PA, "PA") if j % 2 == 0 else (PB, "PB")

                for j in (0, 1):
                    qoff, w_, (stb, sk) = piece(j)
                    p.add("tensor",
                          lambda e, t=t, j=j, qoff=qoff, w_=w_, stb=stb:
                          sc_mm(e, t, j, qoff, w_, stb),
                          r=["R2p%d" % t, "R3"], w=[sk + "0", sk + "1", sk])
                for j in range(njs):
                    qoff, w_, (stb, sk) = piece(j)
                    po = 1024 * (j % 2)
                    pkk = "R6p" + str(j % 2)

                    def ef(e, qoff=qoff, stb=stb, po=po):
                        if qoff == 0:
                            return e.activation(R6p[:, po:po + 1024], stb[:, 0:1024],
                                                AF.Exp, scale=0.125)
                        src_ = stb[:, 0:1024].rearrange("q (bb z) -> q bb z", bb=2)[:, :, qoff:CHUNK]
                        dst = R6p[:, po:po + 1024].rearrange(
                            "q (bb z) -> q bb z", bb=2)[:, :, qoff:CHUNK]
                        return e.activation(dst, src_, AF.Exp, scale=0.125)
                    p.add("scalar", ef, r=[sk, sk + "0", sk + "1"], w=[pkk])
                    if 128 * j >= CHUNK * c:
                        def mkf(e, qoff=qoff, po=po):
                            dst = R6p[:, po:po + 1024].rearrange(
                                "q (bb z) -> q bb z", bb=2)[:, :, qoff:qoff + 128]
                            src_ = SM["tri"][:, :].rearrange("q (bb z) -> q bb z", bb=2)
                            return e.tensor_tensor(dst, dst, src_, ALU.mult)
                        p.add("vector", mkf, r=["tri", pkk], w=[pkk])
                    jn = j + 2
                    if FUSE_PV and jn < njs:
                        qoff2, w2_, (stb2, sk2_) = piece(jn)

                        def fused(e, t=t, j=j, jn=jn, qoff=qoff, w_=w_, po=po,
                                  first=(j == 0), lastj=(j == njs - 1),
                                  qoff2=qoff2, w2_=w2_, stb2=stb2):
                            pv_mm(e, t, j, qoff, w_, po, first, lastj)
                            return sc_mm(e, t, jn, qoff2, w2_, stb2)
                        p.add("tensor", fused, r=[pkk, "R4", "R2p%d" % t, "R3"],
                              w=["PC0", "PC1", sk2_ + "0", sk2_ + "1", sk2_])
                    else:
                        if not FUSE_PV and jn < njs:
                            qoff2, w2_, (stb2, sk2_) = piece(jn)
                            p.add("tensor",
                                  lambda e, jn=jn, qoff2=qoff2, w2_=w2_, stb2=stb2, t=t:
                                  sc_mm(e, t, jn, qoff2, w2_, stb2),
                                  r=["R2p%d" % t, "R3"], w=[sk2_ + "0", sk2_ + "1", sk2_])
                        p.add("tensor",
                              lambda e, t=t, j=j, qoff=qoff, w_=w_, po=po,
                              first=(j == 0), lastj=(j == njs - 1):
                              pv_mm(e, t, j, qoff, w_, po, first, lastj),
                              r=[pkk, "R4"], w=["PC0", "PC1"])
                if mid_hooks and t in mid_hooks:
                    mid_hooks[t]()
                dep_tt = tail_tt.get(t)
                p.add("vector", lambda e, t=t: e.tensor_copy(
                    R6c[0:64, t * CHUNK:(t + 1) * CHUNK], PC[0:64, 0:CHUNK]),
                    r=["PC0"], w=["R6c", "R6tab"], deps=[dep_tt])
                p.add("vector", lambda e, t=t: e.tensor_copy(
                    R6c[64:128, t * CHUNK:(t + 1) * CHUNK], PC[0:64, CHUNK:2 * CHUNK]),
                    r=["PC1"], w=["R6c"], deps=[dep_tt])
                p.add("vector", lambda e: e.tensor_copy(
                    R10b[64:65, 0:2 * CHUNK], PC[64:65, 0:2 * CHUNK]),
                    r=["PC0", "PC1", "R10b"], w=["R7tA"])
                p.add("sync", lambda e, t=t, pi=pi: e.dma_start(
                    R10b[pi + 2 * t:pi + 2 * t + 1, 0:CHUNK], R10b[64:65, 0:CHUNK]),
                    r=["R7tA", "R10b"], w=["R10bs" + str(c % 2)],
                    deps=[tail_beta[0]], dma=True)
                p.add("sync", lambda e, t=t, pi=pi: e.dma_start(
                    R10b[pi + 2 * t + 1:pi + 2 * t + 2, 0:CHUNK],
                    R10b[64:65, CHUNK:2 * CHUNK]),
                    r=["R7tA", "R10b"], w=["R10bs" + str(c % 2)],
                    deps=[tail_beta[0]], dma=True)
                if hooks and t in hooks:
                    hooks[t]()

        def emit_tailA_beta(c):
            pi = 32 * (c % 2)
            sk2 = "R10bs" + str(c % 2)
            kln = "R10bln" + str(c % 2)
            BHI = R10bb[pi:pi + 16, 1024:1024 + CHUNK]
            BLO = R10bb[pi:pi + 16, 1024 + CHUNK:1024 + 2 * CHUNK]
            p.add("scalar", lambda e, pi=pi: e.activation(
                R10b[pi:pi + 16, CHUNK:2 * CHUNK], R10b[pi:pi + 16, 0:CHUNK], AF.Ln,
                bias=SM["epssm"][0:16, 0:1]),
                r=[sk2, "epssm"], w=[kln], hard=True)
            p.add("scalar", lambda e, pi=pi: e.activation(
                R10b[pi:pi + 16, 0:CHUNK], R10b[pi:pi + 16, CHUNK:2 * CHUNK],
                AF.Exp, scale=-1.0),
                r=[kln], w=[sk2], hard=True)

            if b == 0 and c == 3:
                if "d_ctx3" in dbg_out:
                    p.add("sync", lambda e: e.dma_start(dbg_out["d_ctx3"][:, :], R6c[:, :]),
                          r=["R6c"], dma=True)
                if "d_beta3" in dbg_out:
                    p.add("sync", lambda e, pi=pi: e.dma_start(
                        dbg_out["d_beta3"][:, :], R10b[pi:pi + 16, 0:CHUNK]),
                        r=[sk2], dma=True)

            p.add("scalar", lambda e, pi=pi, BHI=BHI: e.activation(
                BHI, R10b[pi:pi + 16, 0:CHUNK], AF.Identity),
                r=[sk2], w=["bhi", kln], hard=True)
            tail_beta[0] = p.add("vector", lambda e, pi=pi, BHI=BHI, BLO=BLO: e.tensor_sub(
                BLO, R10b[pi:pi + 16, 0:CHUNK], BHI),
                r=[sk2, "bhi"], w=["blo", kln], hard=True)
            return pi, BHI, BLO

        def emit_tailA_bcf(c, pi, BHI, BLO, t):
            pbc, pck = (PD, "PD") if t % 2 == 0 else (PE2, "PE2")

            def bcf(e, t=t, pbc=pbc, pi=pi, BHI=BHI, BLO=BLO):
                e.matmul(pbc[:, 0:CHUNK], SM["bsel"][pi:pi + 16, t * 128:(t + 1) * 128],
                         BHI, start=True, stop=False)
                return e.matmul(pbc[:, 0:CHUNK], SM["bsel"][pi:pi + 16, t * 128:(t + 1) * 128],
                                BLO, start=False, stop=True)
            p.add("tensor", bcf, r=["bhi", "blo", "bsel"], w=[pck])
            tail_tt[t] = p.add("vector", lambda e, t=t, pbc=pbc, pck=pck: e.tensor_tensor(
                R7f[:, t * CHUNK:(t + 1) * CHUNK],
                R6c[:, t * CHUNK:(t + 1) * CHUNK],
                pbc[:, 0:CHUNK], ALU.mult),
                r=["R6c", pck], w=["R7f", "R7tab"])

        def emit_tailA(c):
            pi, BHI, BLO = emit_tailA_beta(c)
            for t in range(NPAIR):
                emit_tailA_bcf(c, pi, BHI, BLO, t)

            if b == 0 and c == 3 and "d_ctxf3" in dbg_out:
                p.add("sync", lambda e: e.dma_start(dbg_out["d_ctxf3"][:, :], R7f[:, :]),
                      r=["R7f"], dma=True)

        def emit_tailB(c):
            for et in range(NCT):
                pp, pk = (PD, "PD") if et % 2 == 0 else (PE2, "PE2")

                def opf(e, et=et, pp=pp):
                    for pt in range(NCT):
                        m = e.matmul(pp[:, 0:512],
                                     R5[:, pt * D + et * 128:pt * D + (et + 1) * 128],
                                     R7f[:, pt * CHUNK:(pt + 1) * CHUNK],
                                     start=(pt == 0), stop=(pt == NCT - 1))
                    return m
                p.add("tensor", opf, r=["R5", "R7f"], w=[pk])
                p.add("vector", lambda e, et=et, pp=pp: e.tensor_scalar_add(
                    R9[:, et * 512:(et + 1) * 512], pp[:, 0:512],
                    SM["boc"][:, et:et + 1]),
                    r=[pk, "boc"], w=["R9q" + str(et // 2)])

            for u in range(4):
                st = 4 * c + u
                pbj, pjk = (PDb, "PD") if u % 2 == 0 else (PE2b, "PE2")

                def trf(e, u=u, pbj=pbj):
                    for et in range(NCT):
                        m = e.transpose(pbj[:, et * 128:(et + 1) * 128],
                                        R9[:, et * 512 + u * 128: et * 512 + (u + 1) * 128],
                                        SM["identb"][:, :])
                    return m
                p.add("tensor", trf, r=["R9q0", "R9q1", "R9q2", "R9q3", "identb"], w=[pjk])
                for qi in range(4):
                    eng = ("sync", "scalar", "sync", "scalar")[qi]
                    res = ("R8aL", "R8aL2", "R8aH", "R8aH2")[qi]
                    p.add(eng, lambda e, b=b, c=c, u=u, qi=qi: e.dma_start(
                        R8a[qi * 32:(qi + 1) * 32, :],
                        x_in[b, c * CHUNK + u * 128 + qi * 32:
                             c * CHUNK + u * 128 + (qi + 1) * 32, :]),
                        w=[res], dma=True)
                p.add("vector", lambda e, pbj=pbj: e.tensor_tensor(
                    R8b[:, :], R8a[:, :], pbj[:, 0:1024], ALU.add),
                    r=["R8aL", "R8aL2", "R8aH", "R8aH2"] + [pjk], w=["R8bL", "R8bL2", "R8bH", "R8bH2"])
                p.add("scalar", lambda e, c=c, u=u: e.dma_start(
                    x1_scr[c * CHUNK + u * 128: c * CHUNK + (u + 1) * 128, :], R8b[:, :]),
                    r=["R8bL", "R8bL2", "R8bH", "R8bH2"], w=["x1s" + str(st)], dma=True)

                def stf2(e):
                    e.bn_stats(SM["stats"][:, 0:6], R8b[:, 0:512])
                    return e.bn_stats(SM["stats"][:, 6:12], R8b[:, 512:1024])
                p.add("vector", stf2, r=["R8bL", "R8bL2", "R8bH", "R8bH2"], w=["stats0"])
                p.add("vector", lambda e: e.bn_aggr(SM["mv"][:, 0:2], SM["stats"][:, 0:12]),
                      r=["stats0"], w=["mv0"], hard=True)
                p.add("scalar", lambda e: e.activation(
                    SM["ln1"][:, 2:3], SM["mv"][:, 1:2], AF.Ln, bias=SM["epsln"][:, 0:1]),
                    r=["mv0", "epsln"], w=["ln1x0"], hard=True)
                p.add("scalar", lambda e, st=st: e.activation(
                    SM["r2"][:, st:st + 1], SM["ln1"][:, 2:3], AF.Exp, scale=-0.5),
                    r=["ln1x0"], w=["r2"], hard=True)
                p.add("vector", lambda e, st=st: e.scalar_tensor_tensor(
                    SM["m2r2"][:, st:st + 1], SM["mv"][:, 0:1], -1.0, SM["r2"][:, st:st + 1],
                    ALU.mult, ALU.mult), r=["mv0", "r2"], w=["m2r2"], hard=True)
                p.add("scalar", lambda e, st=st: e.activation(
                    SM["xln2"][:, :], R8b[:, :], AF.Identity,
                    bias=SM["m2r2"][:, st:st + 1], scale=SM["r2"][:, st:st + 1]),
                    r=["R8bL", "R8bL2", "R8bH", "R8bH2"] + ["r2", "m2r2"],
                    w=["xln20"], hard=True)
                pbx, pxk = (PE2b, "PE2") if u % 2 == 0 else (PDb, "PD")

                def txf(e, pbx=pbx):
                    for ct in range(NCT):
                        m = e.transpose(pbx[:, ct * 128:(ct + 1) * 128],
                                        SM["xln2"][:, ct * 128:(ct + 1) * 128],
                                        SM["identb"][:, :])
                    return m
                p.add("tensor", txf, r=["xln20", "identb"], w=[pxk])
                p.add("vector", lambda e, u=u, pbx=pbx: e.tensor_copy(
                    X1L[:, :].rearrange("q (ct z) -> q ct z", ct=NCT)[:, :, u * 128:(u + 1) * 128],
                    pbx[:, 0:1024].rearrange("q (ct z) -> q ct z", ct=NCT)),
                    r=[pxk], w=["X1L"])
                if c == 3 and u == 3:
                    p.add("sync", lambda e: e.dma_start(R10a[0:1, 0:1024], R8b[127:128, :]),
                          r=["R8bL", "R8bL2", "R8bH", "R8bH2"], w=["R10a"], dma=True)
            p.add("vector", lambda e, c=c: e.tensor_scalar_add(
                SM["r2p1"][:, 4 * c:4 * c + 4], SM["r2"][:, 4 * c:4 * c + 4], 1.0),
                r=["r2"], w=["r2p1"], hard=True)
            p.add("scalar", lambda e, c=c: e.dma_start(
                xlt_scr[:, :, c * CHUNK:(c + 1) * CHUNK].rearrange("ct q z -> q ct z"),
                X1L[:, :].rearrange("q (ct z) -> q ct z", ct=NCT)),
                r=["X1L"], w=["xlt" + str(c)], dma=True)

        def router_stages():
          def s1():
            def rtr_t1(e):
                for ct in range(NCT):
                    m = e.transpose(PD[:, ct:ct + 1], R10a[0:1, ct * 128:(ct + 1) * 128],
                                    SM["identf"][0:1, 0:1])
                return m
            p.add("tensor", rtr_t1, r=["R10a", "identf"], w=["PD"])
            p.add("scalar", lambda e: e.copy(SM["lastT"][:, :], PD[:, 0:NCT]),
                  r=["PD"], w=["lastT"], hard=True)
            p.add("sync", lambda e: e.dma_start(SM["sc1"][0:1, 0:1], SM["r2"][127:128, 15:16]),
                  r=["r2"], w=["sc1"], dma=True)
            p.add("sync", lambda e: e.dma_start(SM["sc1"][0:1, 1:2], SM["m2r2"][127:128, 15:16]),
                  r=["m2r2"], w=["sc1"], dma=True)
            p.add("tensor", lambda e: e.matmul(
                PE2[:, 0:2], SM["ones1f"][:, :], SM["sc1"][0:1, 0:2], start=True, stop=True),
                r=["sc1", "ones1f"], w=["PE2"])
            p.add("scalar", lambda e: e.copy(SM["bc1"][:, :], PE2[:, 0:2]),
                  r=["PE2"], w=["bc1"], hard=True)
            p.add("scalar", lambda e: e.activation(
                SM["lastln"][:, :], SM["lastT"][:, :], AF.Identity,
                bias=SM["bc1"][:, 1:2], scale=SM["bc1"][:, 0:1]),
                r=["lastT", "bc1"], w=["lastln"], hard=True)
            if b == 0 and "d_lastln" in dbg_out:
                p.add("sync", lambda e: e.dma_start(dbg_out["d_lastln"][:, :], SM["lastln"][:, :]),
                      r=["lastln"], dma=True)
            p.add("sync", lambda e: e.dma_start(
                R10a[:, :].rearrange("q (ct n) -> q ct n", ct=NCT),
                W["Wp1"].ap().rearrange("(ct q) n -> q ct n", q=128)),
                w=["R10a"], dma=True)

          def s2():
            def rtr_mm(e):
                for ct in range(NCT):
                    m = e.matmul(PD[:, 0:1], R10a[:, ct * 128:(ct + 1) * 128],
                                 SM["lastln"][:, ct:ct + 1], start=(ct == 0),
                                 stop=(ct == NCT - 1))
                return m
            p.add("tensor", rtr_mm, r=["R10a", "lastln"], w=["PD"])
            p.add("scalar", lambda e: e.activation(
                SM["g"][:, :], PD[:, 0:1], AF.Gelu, bias=SM["bp1"][:, 0:1]),
                r=["PD", "bp1"], w=["g"], hard=True)
            p.add("tensor", lambda e: e.matmul(
                PE2[0:NEXP, 0:1], SM["wp2"][:, :], SM["g"][:, :], start=True, stop=True),
                r=["wp2", "g"], w=["PE2"])
            p.add("scalar", lambda e: e.activation(
                SM["logc"][:, :], PE2[0:NEXP, 0:1], AF.Identity, bias=SM["bp2"][:, 0:1]),
                r=["PE2", "bp2"], w=["logc"], hard=True)

          def s3():
            p.add("tensor", lambda e: e.transpose(
                PD[0:1, 0:NEXP], SM["logc"][:, :], SM["identf"][0:NEXP, 0:NEXP]),
                r=["logc", "identf"], w=["PD"])
            p.add("scalar", lambda e: e.copy(SM["row"][:, :], PD[0:1, 0:NEXP]),
                  r=["PD"], w=["row"], hard=True)
            p.add("vector", lambda e: e.tensor_reduce(
                SM["m"][:, :], SM["row"][:, :], axis=AX.X, op=ALU.max),
                r=["row"], w=["mr"], hard=True)
            p.add("vector", lambda e: e.tensor_scalar(
                SM["row2"][:, :], SM["row"][:, :], SM["m"][0:1, 0:1], -1.0,
                ALU.subtract, ALU.mult), r=["row", "mr"], w=["row2"], hard=True)
            p.add("vector", lambda e: e.scalar_tensor_tensor(
                SM["row3"][:, :], SM["iota8"][:, :], 1e-20, SM["row2"][:, :],
                ALU.mult, ALU.add),
                r=["row2", "iota8"], w=["row3"], hard=True)
            p.add("vector", lambda e: e.tensor_reduce(
                SM["umin"][:, :], SM["row3"][:, :], axis=AX.X, op=ALU.min),
                r=["row3"], w=["umin"], hard=True)
            p.add("vector", lambda e: e.tensor_single_scalar(
                SM["row2"][:, :], SM["row3"][:, :], SM["umin"][0:1, 0:1], ALU.is_equal),
                r=["row3", "umin"], w=["row2"], hard=True)
            p.add("vector", lambda e: e.tensor_mul(
                SM["row3"][:, :], SM["row2"][:, :], SM["iota8"][:, :]),
                r=["row2", "iota8"], w=["row3"], hard=True)
            p.add("vector", lambda e: e.reduce_sum(
                SM["self"][:, :], SM["row3"][:, :], axis=AX.X), r=["row3"], w=["self"], hard=True)
            p.add("vector", lambda e: e.tensor_copy(SM["seli"][:, :], SM["self"][:, :]),
                  r=["self"], w=["seli"], hard=True)
            if b == 0 and "d_row" in dbg_out:
                p.add("sync", lambda e: e.dma_start(dbg_out["d_row"][:, :], SM["row"][:, :]),
                      r=["row"], dma=True)
            if b == 0 and "d_sel" in dbg_out:
                p.add("sync", lambda e: e.dma_start(dbg_out["d_sel"][:, :], SM["seli"][:, :]),
                      r=["seli"], dma=True)

          def s4():
            def g_w1(e):
                e.reg_load(greg1, SM["seli"][0:1, 0:1])
                e.reg_mul(greg1, greg1, D * FF)
                return e.dma_start(
                    R1[:, :].rearrange("q (ct f) -> q ct f", ct=NCT),
                    bass.AP(W["W1"], greg1, [[FF, 128], [128 * FF, NCT], [1, FF]]))
            p.add("gpsimd", g_w1, r=["seli"], w=["R1", "R1h0", "R1h1"], dma=True)

            def g_b1(e):
                e.reg_load(greg1, SM["seli"][0:1, 0:1])
                e.reg_mul(greg1, greg1, FF)
                return e.dma_start(SMb1g[:, :], bass.AP(W["b1"], greg1, [[128, 16], [1, 128]]))
            p.add("gpsimd", g_b1, r=["seli"], w=["b1g", "X1L"], dma=True)

            def g_b2(e):
                e.reg_load(greg1, SM["seli"][0:1, 0:1])
                e.reg_mul(greg1, greg1, D)
                return e.dma_start(SMb2g[:, :], bass.AP(W["b2"], greg1, [[128, NEXP], [1, 128]]))
            p.add("gpsimd", g_b2, r=["seli"], w=["b2g", "X1L"], dma=True)

          def s5():
            p.add("tensor", lambda e: e.transpose(
                PE2[:, 0:16], SMb1g[:, :], SM["identf"][0:16, 0:16]),
                r=["b1g", "identf"], w=["PE2"])
            p.add("scalar", lambda e: e.copy(SM["b1c"][:, :], PE2[:, 0:16]),
                  r=["PE2"], w=["b1c"], hard=True)
            p.add("tensor", lambda e: e.transpose(
                PE2[:, 0:NEXP], SMb2g[:, :], SM["identf"][0:NEXP, 0:NEXP]),
                r=["b2g", "identf"], w=["PE2"])
            p.add("scalar", lambda e: e.copy(SM["b2c"][:, :], PE2[:, 0:NEXP]),
                  r=["PE2"], w=["b2c"], hard=True)
          return {0: s1, 1: s2, 2: s3, 3: s4, 5: s5}

        XIN0 = NCT * D

        def emit_ffn_load(k):
            xo = XIN0 + (k % 2) * (NCT * CHUNK)
            xka, xkb = "R4x%da" % (k % 2), "R4x%db" % (k % 2)
            toksR4 = p.cur("R4")
            p.add("sync", lambda e, k=k, xo=xo: e.dma_start(
                R4[:, xo:xo + 4 * CHUNK].rearrange("q (ct z) -> q ct z", ct=4),
                xlt_scr[0:4, :, k * CHUNK:(k + 1) * CHUNK].rearrange("ct q z -> q ct z")),
                r=["xlt" + str(k)], w=[xka, "R4"], dma=True)
            p.add("scalar", lambda e, k=k, xo=xo: e.dma_start(
                R4[:, xo + 4 * CHUNK: xo + 8 * CHUNK].rearrange("q (ct z) -> q ct z", ct=4),
                xlt_scr[4:8, :, k * CHUNK:(k + 1) * CHUNK].rearrange("ct q z -> q ct z")),
                r=["xlt" + str(k)], w=[xkb], deps=toksR4, dma=True)

        def emit_ffn(k, pre=None):
            xo = XIN0 + (k % 2) * (NCT * CHUNK)
            xk = ["R4x%da" % (k % 2), "R4x%db" % (k % 2)]
            for ft in range(16):
                hh = (ft % 2) * 512
                hk = "PA" + str(ft % 2)

                def hf(e, ft=ft, hh=hh, xo=xo):
                    for ct in range(NCT):
                        m = e.matmul(PA[:, hh:hh + 512],
                                     R1[:, ct * FF + ft * 128: ct * FF + (ft + 1) * 128],
                                     R4[:, xo + ct * CHUNK: xo + (ct + 1) * CHUNK],
                                     start=(ct == 0), stop=(ct == NCT - 1))
                    return m
                p.add("tensor", hf, r=["R1"] + xk, w=[hk])
                p.add("scalar", lambda e, ft=ft, hh=hh: e.activation(
                    R3[:, ft * CHUNK:(ft + 1) * CHUNK], PA[:, hh:hh + 512], AF.Gelu,
                    bias=SM["b1c"][:, ft:ft + 1]), r=[hk, "b1c"], w=["R3h" + str(ft % 4)])
            if pre is not None:
                emit_ffn_load(pre)
            if b == 0 and k == 0 and "d_h0" in dbg_out:
                p.add("sync", lambda e: e.dma_start(dbg_out["d_h0"][:, :], R3[:, 0:NH * CHUNK]),
                      r=["R3h0", "R3h1", "R3h2", "R3h3"], dma=True)
            for ht in range(NCT):
                rr = (ht % 2) * 512
                rk = "PB" + str(ht % 2)

                def rfx(e, ht=ht, rr=rr):
                    for ft in range(16):
                        m = e.matmul(PB[:, rr:rr + 512],
                                     R2[:, ft * D + ht * 128: ft * D + (ht + 1) * 128],
                                     R3[:, ft * CHUNK:(ft + 1) * CHUNK],
                                     start=(ft == 0), stop=(ft == 15))
                    return m
                p.add("tensor", rfx,
                      r=["R2p%d" % i for i in range(NPAIR)] + ["R3h0", "R3h1", "R3h2", "R3h3"],
                      w=[rk])
                p.add("vector", lambda e, ht=ht, rr=rr: e.tensor_scalar_add(
                    R3[:, NH * CHUNK + ht * CHUNK: NH * CHUNK + (ht + 1) * CHUNK],
                    PB[:, rr:rr + 512], SM["b2c"][:, ht:ht + 1]),
                    r=[rk, "b2c"], w=["R3r"])
            for et in range(NCT):
                oo = (et % 2) * 512
                ok = "PC" + str(et % 2)

                def of(e, et=et, oo=oo):
                    for ht in range(NCT):
                        m = e.matmul(PC[:, oo:oo + 512],
                                     R4[:, ht * D + et * 128: ht * D + (et + 1) * 128],
                                     R3[:, NH * CHUNK + ht * CHUNK:
                                        NH * CHUNK + (ht + 1) * CHUNK],
                                     start=(ht == 0), stop=(ht == NCT - 1))
                    return m
                p.add("tensor", of, r=["R4", "R4w1", "R4w2", "R3r"], w=[ok])
                p.add("vector", lambda e, et=et, oo=oo: e.tensor_scalar_add(
                    R9[:, et * 512:(et + 1) * 512], PC[:, oo:oo + 512],
                    SM["boFc"][:, et:et + 1]),
                    r=[ok, "boFc"], w=["R9q" + str(et // 2)])
            for u in range(4):
                st = 4 * k + u
                pbj, pjk = (PDb, "PD") if u % 2 == 0 else (PE2b, "PE2")

                def trf2(e, u=u, pbj=pbj):
                    for et in range(NCT):
                        m = e.transpose(pbj[:, et * 128:(et + 1) * 128],
                                        R9[:, et * 512 + u * 128: et * 512 + (u + 1) * 128],
                                        SM["identb"][:, :])
                    return m
                p.add("tensor", trf2, r=["R9q0", "R9q1", "R9q2", "R9q3", "identb"], w=[pjk])
                for qi in range(4):
                    eng = ("sync", "scalar", "sync", "scalar")[qi]
                    res = ("R8aL", "R8aL2", "R8aH", "R8aH2")[qi]
                    p.add(eng, lambda e, k=k, u=u, qi=qi: e.dma_start(
                        R8a[qi * 32:(qi + 1) * 32, :],
                        x1_scr[k * CHUNK + u * 128 + qi * 32:
                               k * CHUNK + u * 128 + (qi + 1) * 32, :]),
                        r=["x1s" + str(st)], w=[res], dma=True)
                p.add("scalar", lambda e, st=st: e.activation(
                    R8b[:, :], R8a[:, :], AF.Identity, bias=SM["m2r2"][:, st:st + 1],
                    scale=SM["r2p1"][:, st:st + 1]),
                    r=["R8aL", "R8aL2", "R8aH", "R8aH2"] + ["m2r2", "r2p1"], w=["R8bL", "R8bL2", "R8bH", "R8bH2"])
                p.add("vector", lambda e, pbj=pbj: e.scalar_tensor_tensor(
                    R8b[:, :], pbj[:, 0:1024], 0.3, R8b[:, :], ALU.mult, ALU.add),
                    r=[pjk] + ["R8bL", "R8bL2", "R8bH", "R8bH2"], w=["R8bL", "R8bL2", "R8bH", "R8bH2"])
                p.add("scalar", lambda e, b=b, k=k, u=u: e.dma_start(
                    out_d[b, k * CHUNK + u * 128: k * CHUNK + (u + 1) * 128, :], R8b[:, :]),
                    r=["R8bL", "R8bL2", "R8bH", "R8bH2"], dma=True)

        emit_attn(3)
        beta3 = emit_tailA_beta(3)
        emit_attn(0, mid_hooks={
            t: (lambda t=t: emit_tailA_bcf(3, beta3[0], beta3[1], beta3[2], t))
            for t in range(NPAIR)})
        if b == 0 and "d_ctxf3" in dbg_out:
            p.add("sync", lambda e: e.dma_start(dbg_out["d_ctxf3"][:, :], R7f[:, :]),
                  r=["R7f"], dma=True)
        emit_tailB(3)
        beta0 = emit_tailA_beta(0)
        emit_attn(1, hooks=router_stages(), mid_hooks={
            t: (lambda t=t: emit_tailA_bcf(0, beta0[0], beta0[1], beta0[2], t))
            for t in range(NPAIR)})
        emit_tailB(0)
        beta1 = emit_tailA_beta(1)
        def g_w2_half(h):
            def g(e, h=h):
                e.reg_load(greg1, SM["seli"][0:1, 0:1])
                e.reg_mul(greg1, greg1, FF * D)
                e.reg_add(greg1, greg1, h * 8 * 128 * D)
                return e.dma_start(
                    R2[:, h * 8192:(h + 1) * 8192].rearrange("q (ft d) -> q ft d", ft=8),
                    bass.AP(W["W2"], greg1, [[D, 128], [128 * D, 8], [1, D]]))
            p.add("gpsimd", g, r=["seli"],
                  w=["R2p%d" % t for t in range(h * 4, h * 4 + 4)], dma=True)

        def attn2_hook(t):
            emit_tailA_bcf(1, beta1[0], beta1[1], beta1[2], t)
            if t == 3:
                g_w2_half(0)
            elif t == 7:
                g_w2_half(1)
        emit_attn(2, mid_hooks={t: (lambda t=t: attn2_hook(t)) for t in range(NPAIR)})
        emit_tailB(1)
        emit_tailA(2)
        toks_pv = p.cur("R4")
        p.add("sync", lambda e: e.dma_start(R4[:, 0:4096], wtF_scr[:, 0:4096]),
              r=["wtF"], w=["R4w1"], deps=toks_pv, dma=True)
        p.add("scalar", lambda e: e.dma_start(R4[:, 4096:NCT * D], wtF_scr[:, 4096:NCT * D]),
              r=["wtF"], w=["R4w2"], deps=toks_pv, dma=True)
        if b == 0 and "d_w1s" in dbg_out:
            p.add("sync", lambda e: e.dma_start(dbg_out["d_w1s"][:, :], R1[:, :]),
                  r=["R1"], dma=True)
        if b == 0 and "d_x1" in dbg_out:
            p.add("sync", lambda e: e.dma_start(dbg_out["d_x1"][:, :], x1_scr[:, :]),
                  r=[f"x1s{st}" for st in range(16)], dma=True)
        if b == 0 and "d_xlt" in dbg_out:
            p.add("sync", lambda e: e.dma_start(dbg_out["d_xlt"][:, :, :], xlt_scr[:, :, :]),
                  r=["xlt0", "xlt1", "xlt2", "xlt3"], dma=True)

        emit_ffn_load(3)
        emit_ffn(3, pre=0)
        emit_tailB(2)
        emit_ffn(0, pre=1)
        emit_ffn(1, pre=2)
        emit_ffn(2)

    p.emit(nc, semh)
    stack.close()
    return nc, sorted(dbg_out.keys())


_CACHE = {}


def _get_prog(nb, dbg):
    key = (nb, tuple(sorted(dbg)))
    if key not in _CACHE:
        _CACHE[key] = build_program(nb, dbg)
    return _CACHE[key]


def _in_maps(inputs, nb, ncores):
    cst = _host_consts()
    x = np.ascontiguousarray(np.asarray(inputs["L0_attn_x"], dtype=np.float32))
    base = {
        "Wq": inputs["L0_attn_Wq"], "bq": inputs["L0_attn_bq"],
        "Wk": inputs["L0_attn_Wk"], "bk": inputs["L0_attn_bk"],
        "Wv": inputs["L0_attn_Wv"], "bv": inputs["L0_attn_bv"],
        "Wo": inputs["L0_attn_Wo"], "bo": inputs["L0_attn_bo"],
        "Wp1": inputs["L0_ffn_Wp1"], "bp1": inputs["L0_ffn_bp1"],
        "Wp2": inputs["L0_ffn_Wp2"], "bp2": inputs["L0_ffn_bp2"],
        "W1": inputs["L0_ffn_W1"], "b1": inputs["L0_ffn_b1"],
        "W2": inputs["L0_ffn_W2"], "b2": inputs["L0_ffn_b2"],
        "WoF": inputs["L0_ffn_Wo"], "boF": inputs["L0_ffn_bo"],
    }
    base = {k: np.ascontiguousarray(np.asarray(v, dtype=np.float32)) for k, v in base.items()}
    base.update(cst)
    maps = []
    for cc in range(ncores):
        m = dict(base)
        m["x"] = np.ascontiguousarray(x[cc * nb:(cc + 1) * nb])
        maps.append(m)
    return maps


def kernel(**inputs):
    nc, _ = _get_prog(NB, tuple(DBG))
    maps = _in_maps(inputs, NB, NCORES)
    res = run_bass_kernel_spmd(nc, maps, core_ids=list(range(NCORES)))
    out = np.concatenate([np.asarray(r["out"]) for r in res.results], axis=0)
    return np.ascontiguousarray(out.astype(np.float32))


def _run_debug(inputs, nb=1, dbg=(), ncores=1):
    nc, names = _get_prog(nb, tuple(dbg))
    maps = _in_maps(inputs, nb, ncores)
    res = run_bass_kernel_spmd(nc, maps, core_ids=list(range(ncores)))
    return res.results[0]
